# version 11
# baseline (speedup 1.0000x reference)
import numpy as np
import ml_dtypes
import concourse.bass as bass
import concourse.mybir as mybir
from concourse.bass_utils import run_bass_kernel_spmd
from contextlib import ExitStack

F32 = mybir.dt.float32
BF16 = mybir.dt.bfloat16
AF = mybir.ActivationFunctionType
ALU = mybir.AluOpType
AX = mybir.AxisListType

B, T, C = 4, 4096, 2048
NH, NKV, HD = 16, 4, 128
NE = 8
NCORE = 8
SCALE = 1.0 / float(np.sqrt(HD))
SHIFT = 12.0
EPS = 1e-5
CHUNK_SETS = [[7, 4, 2, 1], [6, 5, 3, 0]]
NK = [8, 6, 4, 2]
NSLOT = 4
QR = 2048
BFD = ml_dtypes.bfloat16


class EQ:
    def __init__(self, queue_name, step=1):
        self.queue = queue_name
        self.step = step
        self.sem = None
        self.n = 0
        self.waited = {}


class Emitter:
    def __init__(self):
        self.eqs = []
        self.mode = None
        self.q = None

    def new_eq(self, queue_name, step=1):
        eq = EQ(queue_name, step)
        self.eqs.append(eq)
        return eq

    def begin(self, mode, q, reset=True):
        self.mode = mode
        self.q = q
        if reset:
            for eq in self.eqs:
                eq.n = 0
                eq.waited = {}

    def _wait(self, eq, waits):
        agg = {}
        for t in waits:
            if t is None:
                continue
            weq, val = t
            if val <= 0:
                continue
            k = id(weq)
            if k not in agg or agg[k][1] < val:
                agg[k] = (weq, val)
        for k, (weq, val) in agg.items():
            if eq.waited.get(k, 0) >= val:
                continue
            self.q.wait_ge(weq.sem, val)
            eq.waited[k] = val

    def op(self, eq, waits, fn, inc=None):
        inc = eq.step if inc is None else inc
        if self.mode == eq.queue:
            self._wait(eq, waits)
            ins = fn(self.q)
            if inc:
                ins.then_inc(eq.sem, inc)
        eq.n += inc
        return (eq, eq.n)

    def wait_only(self, eq, waits):
        if self.mode == eq.queue:
            self._wait(eq, waits)


class Buf:
    def __init__(self):
        self.r = None
        self.u = []

    def wr(self):
        return [t for t in [self.r] + self.u if t is not None]

    def rd(self):
        return [self.r] if self.r is not None else []

    def set_w(self, tok):
        self.r = tok
        self.u = []

    def add_u(self, tok):
        self.u.append(tok)


def build_nc():
    nc = bass.Bass("TRN2", target_bir_lowering=False, debug=False)

    d_xT = nc.dram_tensor("xT", [C, T], F32, kind="ExternalInput").ap()
    d_xqT = nc.dram_tensor("xqT", [C, QR], F32, kind="ExternalInput").ap()
    d_xrows = nc.dram_tensor("xrows", [QR, C], F32, kind="ExternalInput").ap()
    d_cosk = nc.dram_tensor("cosk", [128, T], BF16, kind="ExternalInput").ap()
    d_sink = nc.dram_tensor("sink", [128, T], BF16, kind="ExternalInput").ap()
    d_cosq = nc.dram_tensor("cosq", [128, QR], BF16, kind="ExternalInput").ap()
    d_sinq = nc.dram_tensor("sinq", [128, QR], BF16, kind="ExternalInput").ap()
    d_mk2 = nc.dram_tensor("maskp2", [NSLOT, 8, 128, 512], BF16, kind="ExternalInput").ap()
    d_onec = nc.dram_tensor("ones_c", [128, 1], BF16, kind="ExternalInput").ap()
    d_wq = nc.dram_tensor("wq_t", [16, 128, 2048], F32, kind="ExternalInput").ap()
    d_wk = nc.dram_tensor("wk_t", [128, 8192], F32, kind="ExternalInput").ap()
    d_wv = nc.dram_tensor("wv_t", [128, 8192], F32, kind="ExternalInput").ap()
    d_wp = nc.dram_tensor("wp_t", [16, 128, 2048], F32, kind="ExternalInput").ap()
    d_wr = nc.dram_tensor("wr_t", [128, 128], F32, kind="ExternalInput").ap()
    d_wg = nc.dram_tensor("wg_t", [NE, 16, 128, 2048], F32, kind="ExternalInput").ap()
    d_wu = nc.dram_tensor("wu_t", [NE, 16, 128, 2048], F32, kind="ExternalInput").ap()
    d_wd = nc.dram_tensor("wd_t", [NE, 4, 16, 128, 512], F32, kind="ExternalInput").ap()
    d_ones = nc.dram_tensor("ones_f", [1, 128], F32, kind="ExternalInput").ap()
    d_konst = nc.dram_tensor("konst", [128, 2], F32, kind="ExternalInput").ap()
    d_out = nc.dram_tensor("out_rows", [QR, C], F32, kind="ExternalOutput").ap()
    d_spill = nc.dram_tensor("spill", [QR, C], F32, kind="Internal").ap()
    d_spill_bf = nc.dram_tensor("spill_bf", [QR, C], BF16, kind="Internal").ap()
    d_wqbf = nc.dram_tensor("wq_bf", [16, 128, 2048], BF16, kind="Internal").ap()
    d_wpbf = nc.dram_tensor("wp_bf", [16, 128, 2048], BF16, kind="Internal").ap()

    E = Emitter()
    PE = E.new_eq("PE")
    DVE = E.new_eq("DVE")
    ACT = E.new_eq("ACT")
    s_cst = E.new_eq("PL", 16)
    s_xa = [E.new_eq("SP", 16), E.new_eq("PL", 16), E.new_eq("SP", 16), E.new_eq("PL", 16)]
    s_mk = E.new_eq("PL", 16)
    s_xr = [E.new_eq("PL", 16) for _ in range(2)]
    s_spw = [E.new_eq("PL", 16) for _ in range(2)]
    s_spr = [E.new_eq("PL", 16) for _ in range(2)]
    s_ai = [E.new_eq("PL", 16) for _ in range(2)]
    s_ob = [E.new_eq("PL", 16) for _ in range(2)]
    s_wa = [E.new_eq("SP", 16) for _ in range(2)]
    s_tr = [E.new_eq("SP", 16) for _ in range(2)]
    s_wb = [E.new_eq("ACT", 16) for _ in range(2)]
    s_wg2 = [E.new_eq("PL", 16) for _ in range(2)]
    s_wu2 = [E.new_eq("PL", 16) for _ in range(2)]
    s_wd = [E.new_eq("SP", 16), E.new_eq("PL", 16)]

    top = ExitStack()
    for i, eq in enumerate(E.eqs):
        eq.sem = top.enter_context(nc.semaphore(f"sem{i}"))

    SB = lambda name, shape, dt: nc.sbuf_tensor(name, shape, dt)

    with top:
        ones_sb = top.enter_context(SB("ones_sb", [1, 128], F32))
        onec_sb = top.enter_context(SB("onec_sb", [128, 1], BF16))
        ksb = top.enter_context(SB("ksb", [128, 2], F32))
        esAB = ExitStack()
        kch = esAB.enter_context(SB("kch", [128, 4 * T], BF16))
        vtm = esAB.enter_context(SB("vtm", [128, 32 * 512], BF16))
        xstag = [esAB.enter_context(SB(f"xstag{i}", [128, 512], F32)) for i in range(2)]
        xch = esAB.enter_context(SB("xch", [128, 16 * 512], BF16))
        ropet = [esAB.enter_context(SB(f"ropet{i}", [128, 512], F32)) for i in range(2)]
        W0 = top.enter_context(nc.psum_tensor("W0", [128, 1024], F32))
        W1 = top.enter_context(nc.psum_tensor("W1", [128, 1024], F32))
        S = [top.enter_context(nc.psum_tensor(f"S{i}", [128, 512], F32)) for i in range(4)]

        def bank(i):
            if i == 0:
                return W0[:, 0:512]
            if i == 1:
                return W0[:, 512:1024]
            if i == 2:
                return W1[:, 0:512]
            if i == 3:
                return W1[:, 512:1024]
            return S[i - 4][:]

        def quiesce():
            toks = []
            for s in ([s_cst, s_mk] + s_xa + s_xr + s_spw + s_spr + s_ai + s_wa + s_tr
                      + s_wb + s_wg2 + s_wu2 + s_wd + s_ob):
                toks.append((s, s.n))
            for e in (PE, DVE, ACT):
                E.wait_only(e, toks)
            E.wait_only(s_wa[0], toks)
            E.wait_only(s_cst, toks)

        def mk_rope(PB, b_rt, b_tab):
            def rope(bi, cos_ap, sin_ap, out_ap, out_waits=()):
                pb = PB[bi]
                bk = bank(bi)
                t1, t2 = ropet
                bt1, bt2 = b_rt
                tok = E.op(DVE, pb.rd() + bt1.wr() + b_tab.rd(),
                           lambda q: q.tensor_mul(t1[:], bk, cos_ap))
                bt1.set_w(tok)
                b_tab.add_u(tok)
                E.op(DVE, pb.rd() + bt2.wr(),
                     lambda q: q.tensor_mul(t2[0:64, :], bk[64:128, :], sin_ap[64:128]))
                tok2 = E.op(DVE, pb.rd(),
                            lambda q: q.tensor_mul(t2[64:128, :], bk[0:64, :], sin_ap[0:64]))
                bt2.set_w(tok2)
                pb.add_u(tok2)
                b_tab.add_u(tok2)
                wa = bt1.rd() + bt2.rd() + list(out_waits)
                E.op(DVE, wa, lambda q: q.tensor_sub(out_ap[0:64], t1[0:64, :], t2[0:64, :]))
                tb = E.op(DVE, wa, lambda q: q.tensor_add(out_ap[64:128], t1[64:128, :], t2[64:128, :]))
                bt1.add_u(tb)
                bt2.add_u(tb)
                return tb
            return rope

        stA = {}
        esA = ExitStack()
        stA["tabk_c"] = esA.enter_context(SB("tabk_c", [128, T], BF16))
        stA["tabk_s"] = esA.enter_context(SB("tabk_s", [128, T], BF16))
        stA["wkb"] = esA.enter_context(SB("wkb", [128, 8192], BF16))
        stA["wvb"] = esA.enter_context(SB("wvb", [128, 8192], BF16))
        stA["wkstag"] = [esA.enter_context(SB(f"wkstag{i}", [128, 4096], F32)) for i in range(2)]

        def walk_A():
            PB = [Buf() for _ in range(8)]
            b_cst = Buf()
            b_stag = [Buf() for _ in range(4)]
            b_xch = Buf()
            b_rt = [Buf(), Buf()]
            b_tab = Buf()
            b_wkb, b_wvb = Buf(), Buf()
            b_ws = [Buf(), Buf()]
            tabk_c, tabk_s = stA["tabk_c"], stA["tabk_s"]
            wkb, wvb, wkstag = stA["wkb"], stA["wvb"], stA["wkstag"]

            for dst, src in [(tabk_c[:], d_cosk), (tabk_s[:], d_sink),
                             (ones_sb[:], d_ones),
                             (onec_sb[:], d_onec), (ksb[:], d_konst)]:
                E.op(s_cst, [], lambda q, dst=dst, src=src: q.dma_start(out=dst, in_=src))
            E.op(s_cst, [], lambda q: q.dma_start(out=wkstag[0][:], in_=d_wk[:, 0:4096]))
            tok = E.op(s_cst, [], lambda q: q.dma_start(out=wkstag[1][:], in_=d_wk[:, 4096:8192]))
            b_cst.set_w(tok)
            b_tab.set_w(tok)
            E.op(DVE, b_cst.rd(), lambda q: q.tensor_copy(wkb[:, 0:4096], wkstag[0][:]))
            t = E.op(DVE, b_cst.rd(), lambda q: q.tensor_copy(wkb[:, 4096:8192], wkstag[1][:]))
            b_wkb.set_w(t)
            b_ws[0].set_w(t)
            b_ws[1].set_w(t)
            E.op(s_cst, b_ws[0].wr(), lambda q: q.dma_start(out=wkstag[0][:], in_=d_wv[:, 0:4096]))
            tok = E.op(s_cst, b_ws[1].wr(), lambda q: q.dma_start(out=wkstag[1][:], in_=d_wv[:, 4096:8192]))
            b_ws[0].set_w(tok)
            b_ws[1].set_w(tok)
            tv0 = E.op(DVE, b_ws[0].rd(), lambda q: q.tensor_copy(wvb[:, 0:4096], wkstag[0][:]))
            t = E.op(DVE, b_ws[1].rd(), lambda q: q.tensor_copy(wvb[:, 4096:8192], wkstag[1][:]))
            b_wvb.set_w(t)
            stag4 = [xstag[0][:], xstag[1][:], wkstag[0][:, 0:512], wkstag[1][:, 0:512]]
            b_stag[2].set_w(tv0)
            b_stag[3].set_w(t)

            rope = mk_rope(PB, b_rt, b_tab)

            for tc in range(8):
                conv = []
                for ct in range(16):
                    si = ct % 4
                    tok = E.op(s_xa[si], b_stag[si].wr(),
                               lambda q, ct=ct, tc=tc, si=si: q.dma_start(
                                   out=stag4[si],
                                   in_=d_xT[ct * 128:(ct + 1) * 128, tc * 512:(tc + 1) * 512]))
                    b_stag[si].set_w(tok)
                    w = b_stag[si].rd() + (b_xch.wr() if ct == 0 else [])
                    tok2 = E.op(DVE, w, lambda q, ct=ct, si=si: q.tensor_copy(
                        xch[:, ct * 512:(ct + 1) * 512], stag4[si]))
                    b_stag[si].add_u(tok2)
                    conv.append(tok2)
                b_xch.set_w(conv[-1])

                for g in range(4):
                    bi = g % 2
                    pb = PB[bi]
                    last = None
                    for ct in range(16):
                        w = (pb.wr() + b_wkb.rd() + [conv[15]]) if ct == 0 else []
                        last = E.op(PE, w, lambda q, g=g, ct=ct, bi=bi: q.matmul(
                            bank(bi), wkb[:, (g * 16 + ct) * 128:(g * 16 + ct + 1) * 128],
                            xch[:, ct * 512:(ct + 1) * 512],
                            start=(ct == 0), stop=(ct == 15)), inc=(1 if ct == 15 else 0))
                    pb.set_w(last)
                    b_wkb.add_u(last)
                    b_xch.add_u(last)
                    rope(bi, tabk_c[:, tc * 512:(tc + 1) * 512],
                         tabk_s[:, tc * 512:(tc + 1) * 512],
                         kch[:, g * T + tc * 512:g * T + (tc + 1) * 512])

                for tt in range(4):
                    bi = 2 + tt % 2
                    pb = PB[bi]
                    last = None
                    for ct in range(16):
                        w = (pb.wr() + b_wvb.rd() + [conv[15]]) if ct == 0 else []
                        last = E.op(PE, w, lambda q, tt=tt, ct=ct, bi=bi: q.matmul(
                            bank(bi), xch[:, ct * 512 + tt * 128:ct * 512 + (tt + 1) * 128],
                            wvb[:, ct * 512:(ct + 1) * 512],
                            start=(ct == 0), stop=(ct == 15)), inc=(1 if ct == 15 else 0))
                    pb.set_w(last)
                    b_wvb.add_u(last)
                    b_xch.add_u(last)
                    tok = E.op(DVE, pb.rd(), lambda q, tt=tt, tc=tc, bi=bi: q.tensor_copy(
                        vtm[:, (tc * 4 + tt) * 512:(tc * 4 + tt + 1) * 512], bank(bi)))
                    pb.add_u(tok)

        def alloc_B():
            st = {}
            es = ExitStack()
            st["_es"] = es
            st["tabq_c"] = es.enter_context(SB("tabq_c", [128, 512], BF16))
            st["tabq_s"] = es.enter_context(SB("tabq_s", [128, 512], BF16))
            st["qch"] = es.enter_context(SB("qch", [128, 16 * 512], BF16))
            st["ptb"] = [es.enter_context(SB(f"ptb{i}", [128, 1024], BF16)) for i in range(3)]
            st["ych"] = es.enter_context(SB("ych", [128, 16 * 512], BF16))
            st["mk2"] = es.enter_context(SB("mk2", [128, 8 * 512], BF16))
            st["wstf"] = [es.enter_context(SB(f"wstf{i}", [128, 2048], F32)) for i in range(3)]
            st["wqb"] = [es.enter_context(SB(f"wqb{i}", [128, 2048], BF16)) for i in range(2)]
            st["x1pre"] = [es.enter_context(SB(f"x1pre{i}", [128, C], F32)) for i in range(2)]
            st["xres"] = [es.enter_context(SB(f"xres{i}", [128, C], F32)) for i in range(2)]
            st["x1bf"] = es.enter_context(SB("x1bf", [128, 1024], BF16))
            st["sesb"] = es.enter_context(SB("sesb", [1, 512], F32))
            st["csb"] = [es.enter_context(SB(f"csb{i}", [128, 512], BF16)) for i in range(2)]
            st["sm"] = es.enter_context(SB("sm", [128, 4], F32))
            st["bnst"] = es.enter_context(SB("bnst", [128, 4, 6], F32))
            st["bnag"] = es.enter_context(SB("bnag", [128, 2], F32))
            return st

        def walk_B(st):
            PB = [Buf() for _ in range(8)]
            tabq_c, tabq_s = st["tabq_c"], st["tabq_s"]
            qch, ych = st["qch"], st["ych"]
            ptb = st["ptb"]
            mk2 = st["mk2"]
            wstf, wqb = st["wstf"], st["wqb"]
            x1pre, xres = st["x1pre"], st["xres"]
            x1bf = st["x1bf"]
            csb, sm = st["csb"], st["sm"]
            sesb = st["sesb"]
            bnst, bnag = st["bnst"], st["bnag"]

            b_tab = Buf()
            b_xch = Buf()
            b_stag = [Buf() for _ in range(2)]
            b_qch = Buf()
            b_ptb = [Buf() for _ in range(3)]
            b_ych = Buf()
            b_mk2 = Buf()
            b_ws = [Buf() for _ in range(3)]
            b_wqb = [Buf(), Buf()]
            b_rt = [Buf(), Buf()]
            b_x1 = [Buf(), Buf()]
            b_xr = [Buf(), Buf()]
            b_se = Buf()
            b_cs = [Buf(), Buf()]
            b_sm = [Buf(), Buf()]
            b_bn = Buf()
            b_x1b = Buf()
            b_kch, b_vtm = Buf(), Buf()

            quiesce()
            rope = mk_rope(PB, b_rt, b_tab)

            for j in range(NSLOT):
                n = NK[j]
                E.op(s_cst, b_tab.wr(), lambda q, j=j: q.dma_start(
                    out=tabq_c[:], in_=d_cosq[:, j * 512:(j + 1) * 512]))
                tok = E.op(s_cst, [], lambda q, j=j: q.dma_start(
                    out=tabq_s[:], in_=d_sinq[:, j * 512:(j + 1) * 512]))
                b_tab.set_w(tok)
                tokm = E.op(s_mk, b_mk2.wr(), lambda q, j=j: q.dma_start(
                    out=mk2[:].rearrange("p (a f) -> p a f", a=8),
                    in_=d_mk2[j].rearrange("a p f -> p a f")))
                b_mk2.set_w(tokm)

                conv = []
                for ct in range(16):
                    si = ct % 2
                    tok = E.op(s_xa[si], b_stag[si].wr(),
                               lambda q, ct=ct, j=j: q.dma_start(
                                   out=xstag[ct % 2][:],
                                   in_=d_xqT[ct * 128:(ct + 1) * 128, j * 512:(j + 1) * 512]))
                    b_stag[si].set_w(tok)
                    w = b_stag[si].rd() + (b_xch.wr() if ct == 0 else [])
                    tok2 = E.op(DVE, w, lambda q, ct=ct: q.tensor_copy(
                        xch[:, ct * 512:(ct + 1) * 512], xstag[ct % 2][:]))
                    b_stag[si].add_u(tok2)
                    conv.append(tok2)
                b_xch.set_w(conv[-1])

                for qc in range(16):
                    ci = qc % 2
                    if j == 0:
                        wi = qc % 3
                        tokA = E.op(s_wa[qc % 2], b_ws[wi].wr(),
                                    lambda q, qc=qc, wi=wi: q.dma_start(
                                        out=wstf[wi][:, 0:1024], in_=d_wq[qc, :, 0:1024]))
                        tokB = E.op(s_wg2[qc % 2], b_ws[wi].wr(),
                                    lambda q, qc=qc, wi=wi: q.dma_start(
                                        out=wstf[wi][:, 1024:2048], in_=d_wq[qc, :, 1024:2048]))
                        b_ws[wi].set_w(tokA)
                        b_ws[wi].add_u(tokB)
                        tok2 = E.op(DVE, [tokA, tokB] + b_wqb[ci].wr(),
                                    lambda q, wi=wi, ci=ci: q.tensor_copy(wqb[ci][:], wstf[wi][:]))
                        b_ws[wi].add_u(tok2)
                        b_wqb[ci].set_w(tok2)
                        tokwb = E.op(s_ob[ci], b_wqb[ci].rd(),
                                     lambda q, qc=qc, ci=ci: q.dma_start(
                                         out=d_wqbf[qc], in_=wqb[ci][:]))
                        b_wqb[ci].add_u(tokwb)
                        ready = [tok2]
                    else:
                        tokA = E.op(s_wa[qc % 2], b_wqb[ci].wr(),
                                    lambda q, qc=qc, ci=ci: q.dma_start(
                                        out=wqb[ci][:, 0:1024], in_=d_wqbf[qc, :, 0:1024]))
                        tokB = E.op(s_wg2[qc % 2], b_wqb[ci].wr(),
                                    lambda q, qc=qc, ci=ci: q.dma_start(
                                        out=wqb[ci][:, 1024:2048], in_=d_wqbf[qc, :, 1024:2048]))
                        b_wqb[ci].set_w(tokA)
                        b_wqb[ci].add_u(tokB)
                        ready = [tokA, tokB]
                    bi = qc % 2
                    pb = PB[bi]
                    last = None
                    for ct in range(16):
                        w = (pb.wr() + ready + [conv[15]] + (b_qch.wr() if qc == 0 else [])) \
                            if ct == 0 else []
                        last = E.op(PE, w, lambda q, ct=ct, bi=bi, ci=ci: q.matmul(
                            bank(bi), wqb[ci][:, ct * 128:(ct + 1) * 128],
                            xch[:, ct * 512:(ct + 1) * 512],
                            start=(ct == 0), stop=(ct == 15)), inc=(1 if ct == 15 else 0))
                    pb.set_w(last)
                    b_wqb[ci].add_u(last)
                    b_xch.add_u(last)
                    tb = rope(bi, tabq_c[:], tabq_s[:],
                              qch[:, qc * 512:(qc + 1) * 512],
                              out_waits=(b_qch.wr() if qc == 0 else []))
                    if qc == 15:
                        b_qch.set_w(tb)

                def epi_a(h):
                    pb7 = PB[7]
                    t1 = E.op(DVE, pb7.rd() + b_se.wr(),
                              lambda q: q.tensor_copy(sesb[:], S[3][0:1, 0:512]))
                    pb7.add_u(t1)
                    b_se.set_w(t1)

                def epi_b(h):
                    par = h % 2
                    yb = 6 if par == 0 else 4
                    pb5 = PB[5]
                    t3 = E.op(PE, pb5.wr() + b_se.rd(),
                              lambda q: q.matmul(S[1][:], ones_sb[:], sesb[:],
                                                 start=True, stop=True))
                    pb5.set_w(t3)
                    b_se.add_u(t3)
                    def _recip_bf16(q, par):
                        with nc.allow_low_precision(reason="bf16 1/se factor"):
                            return q.reciprocal(csb[par][:], S[1][:])
                    t4 = E.op(DVE, pb5.rd() + b_cs[par].wr(),
                              lambda q, par=par: _recip_bf16(q, par))
                    pb5.add_u(t4)
                    b_cs[par].set_w(t4)
                    pbY = PB[yb]
                    tok = E.op(DVE, pbY.rd() + b_cs[par].rd() + (b_ych.wr() if h == 0 else []),
                               lambda q, h=h, yb=yb, par=par: q.tensor_mul(
                                   ych[:, h * 512:(h + 1) * 512], bank(yb), csb[par][:]))
                    pbY.add_u(tok)
                    b_cs[par].add_u(tok)
                    if h == 15:
                        b_ych.set_w(tok)

                def pass2(h, prev):
                    par = h % 2
                    g = h // 4
                    n4 = n * 4
                    yb = 6 if par == 0 else 4
                    P = n4 // 2
                    if prev is not None:
                        epi_a(prev)

                    def stage1(kp):
                        wi = kp % 2
                        pi = kp % 3
                        pbl, pbr = PB[2 * wi], PB[2 * wi + 1]
                        mmtoks = []
                        for half in range(2):
                            kc2 = kp * 2 + half
                            pb = pbl if half == 0 else pbr
                            tok = E.op(PE, pb.wr() + b_qch.rd(),
                                       lambda q, kc2=kc2, half=half, g=g, h=h, wi=wi: q.matmul(
                                           bank(2 * wi + half),
                                           kch[:, g * T + kc2 * 128:g * T + (kc2 + 1) * 128],
                                           qch[:, h * 512:(h + 1) * 512],
                                           start=True, stop=True))
                            pb.set_w(tok)
                            b_qch.add_u(tok)
                            b_kch.add_u(tok)
                            mmtoks.append(tok)
                        tok = E.op(ACT, mmtoks + b_ptb[pi].wr(),
                                   lambda q, wi=wi, pi=pi: q.activation(
                                       ptb[pi][:], (W0 if wi == 0 else W1)[:], AF.Exp,
                                       bias=ksb[:, 0:1], scale=SCALE))
                        pbl.add_u(tok)
                        pbr.add_u(tok)
                        b_ptb[pi].set_w(tok)
                        if kp >= P - 4:
                            mi = kp - (P - 4)
                            tok = E.op(DVE, b_ptb[pi].rd() + b_mk2.rd(),
                                       lambda q, pi=pi, mi=mi: q.tensor_mul(
                                           ptb[pi][:], ptb[pi][:],
                                           mk2[:, mi * 1024:(mi + 1) * 1024]))
                            b_ptb[pi].set_w(tok)
                            b_mk2.add_u(tok)

                    res = {}

                    def stage2(kp):
                        pi = kp % 3
                        pbY = PB[yb]
                        pb7 = PB[7]
                        for half in range(2):
                            kc2 = kp * 2 + half
                            w = b_ptb[pi].rd() + (pbY.wr() if kc2 == 0 else [])
                            ylast = E.op(PE, w, lambda q, kc2=kc2, g=g, pi=pi, half=half, yb=yb: q.matmul(
                                bank(yb), vtm[:, kc2 * 512 + g * 128:kc2 * 512 + (g + 1) * 128],
                                ptb[pi][:, half * 512:(half + 1) * 512],
                                start=(kc2 == 0), stop=(kc2 == n4 - 1)),
                                inc=(1 if half == 1 else 0))
                            b_ptb[pi].add_u(ylast)
                            b_vtm.add_u(ylast)
                            w = b_ptb[pi].rd() + (pb7.wr() if kc2 == 0 else [])
                            selast = E.op(PE, w, lambda q, kc2=kc2, pi=pi, half=half: q.matmul(
                                S[3][0:1, 0:512], onec_sb[:],
                                ptb[pi][:, half * 512:(half + 1) * 512],
                                start=(kc2 == 0), stop=(kc2 == n4 - 1)),
                                inc=(1 if kc2 == n4 - 1 else 0))
                            b_ptb[pi].add_u(selast)
                        res["y"] = ylast
                        res["se"] = selast

                    stage1(0)
                    if prev is not None:
                        epi_b(prev)
                    stage1(1)
                    for kp in range(2, P):
                        stage1(kp)
                        stage2(kp - 2)
                    stage2(P - 2)
                    stage2(P - 1)
                    PB[yb].set_w(res["y"])
                    PB[7].set_w(res["se"])

                for h in range(16):
                    pass2(h, h - 1 if h > 0 else None)
                epi_a(15)
                epi_b(15)

                for tp in range(2):
                    for ti in range(2):
                        tt = tp * 2 + ti
                        tok = E.op(s_xr[ti], b_xr[ti].wr(),
                                   lambda q, tt=tt, j=j, ti=ti: q.dma_start(
                                       out=xres[ti][:],
                                       in_=d_xrows[j * 512 + tt * 128:j * 512 + (tt + 1) * 128, :]))
                        b_xr[ti].set_w(tok)
                    lasts = {}
                    for yc in range(16):
                        ci = yc % 2
                        if j == 0 and tp == 0:
                            wi = yc % 3
                            tokA = E.op(s_wa[yc % 2], b_ws[wi].wr(),
                                        lambda q, yc=yc, wi=wi: q.dma_start(
                                            out=wstf[wi][:, 0:1024], in_=d_wp[yc, :, 0:1024]))
                            tokB = E.op(s_wg2[yc % 2], b_ws[wi].wr(),
                                        lambda q, yc=yc, wi=wi: q.dma_start(
                                            out=wstf[wi][:, 1024:2048], in_=d_wp[yc, :, 1024:2048]))
                            b_ws[wi].set_w(tokA)
                            b_ws[wi].add_u(tokB)
                            tok2 = E.op(DVE, [tokA, tokB] + b_wqb[ci].wr(),
                                        lambda q, wi=wi, ci=ci: q.tensor_copy(wqb[ci][:], wstf[wi][:]))
                            b_ws[wi].add_u(tok2)
                            b_wqb[ci].set_w(tok2)
                            tokwb = E.op(s_ob[ci], b_wqb[ci].rd(),
                                         lambda q, yc=yc, ci=ci: q.dma_start(
                                             out=d_wpbf[yc], in_=wqb[ci][:]))
                            b_wqb[ci].add_u(tokwb)
                            ready = [tok2]
                        else:
                            tokA = E.op(s_wa[yc % 2], b_wqb[ci].wr(),
                                        lambda q, yc=yc, ci=ci: q.dma_start(
                                            out=wqb[ci][:, 0:1024], in_=d_wpbf[yc, :, 0:1024]))
                            tokB = E.op(s_wg2[yc % 2], b_wqb[ci].wr(),
                                        lambda q, yc=yc, ci=ci: q.dma_start(
                                            out=wqb[ci][:, 1024:2048], in_=d_wpbf[yc, :, 1024:2048]))
                            b_wqb[ci].set_w(tokA)
                            b_wqb[ci].add_u(tokB)
                            ready = [tokA, tokB]
                        tok2r = ready
                        for ti in range(2):
                            tt = tp * 2 + ti
                            for oc in range(4):
                                bi = ti * 4 + oc
                                pb = PB[bi]
                                w = tok2r + (pb.wr() + b_ych.rd() if yc == 0 else [])
                                last = E.op(PE, w, lambda q, yc=yc, tt=tt, oc=oc, bi=bi, ci=ci: q.matmul(
                                    bank(bi), ych[:, yc * 512 + tt * 128:yc * 512 + (tt + 1) * 128],
                                    wqb[ci][:, oc * 512:(oc + 1) * 512],
                                    start=(yc == 0), stop=(yc == 15)),
                                    inc=(1 if (ti == 1 and oc == 3) else 0))
                        b_wqb[ci].add_u(last)
                        b_ych.add_u(last)
                    for ti in range(2):
                        tt = tp * 2 + ti
                        for oc in range(4):
                            bi = ti * 4 + oc
                            pb = PB[bi]
                            pb.set_w(last)
                            tok = E.op(DVE, pb.rd() + b_xr[ti].rd() + (b_x1[ti].wr() if oc == 0 else []),
                                       lambda q, ti=ti, oc=oc, bi=bi: q.tensor_add(
                                           x1pre[ti][:, oc * 512:(oc + 1) * 512], bank(bi),
                                           xres[ti][:, oc * 512:(oc + 1) * 512]))
                            pb.add_u(tok)
                            b_xr[ti].add_u(tok)
                        b_x1[ti].set_w(tok)
                        for ch in range(4):
                            tokb = E.op(DVE, b_x1[ti].rd() + (b_bn.wr() if ch == 0 else []),
                                        lambda q, ti=ti, ch=ch: q.bn_stats(
                                            bnst[:, ch, :], x1pre[ti][:, ch * 512:(ch + 1) * 512]))
                        tokb = E.op(DVE, [tokb], lambda q: q.bn_aggr(bnag[:], bnst[:]))
                        b_bn.set_w(tokb)
                        ve = sm[:, 0:1]
                        nm = sm[:, 1:2]
                        rs = sm[:, 2:3]
                        t1 = E.op(DVE, [tokb] + b_sm[0].wr(), lambda q, ve=ve: q.tensor_scalar_add(
                            ve, bnag[:, 1:2], EPS))
                        t2 = E.op(ACT, [t1], lambda q, ve=ve: q.activation(ve, ve, AF.Ln))
                        t3 = E.op(DVE, [t2], lambda q, ve=ve, rs=rs: q.tensor_scalar_mul(rs, ve, -0.5))
                        t4 = E.op(ACT, [t3], lambda q, rs=rs: q.activation(rs, rs, AF.Exp))
                        t5 = E.op(DVE, [tokb], lambda q, nm=nm: q.tensor_scalar_mul(nm, bnag[:, 0:1], -1.0))
                        b_sm[0].set_w(t4)
                        tok = E.op(DVE, [t4, t5] + b_x1[ti].rd(),
                                   lambda q, ti=ti, nm=nm, rs=rs: q.tensor_scalar(
                                       x1pre[ti][:], x1pre[ti][:], nm, rs, ALU.add, ALU.mult))
                        b_x1[ti].set_w(tok)
                        b_bn.add_u(tok)
                        tok = E.op(s_spw[ti], b_x1[ti].rd(),
                                   lambda q, ti=ti, tt=tt, j=j: q.dma_start(
                                       out=d_spill[j * 512 + tt * 128:j * 512 + (tt + 1) * 128, :],
                                       in_=x1pre[ti][:]))
                        b_x1[ti].add_u(tok)
                        for hv in range(2):
                            tokb16 = E.op(DVE, b_x1[ti].rd() + b_x1b.wr(),
                                          lambda q, ti=ti, hv=hv: q.tensor_copy(
                                              x1bf[:], x1pre[ti][:, hv * 1024:(hv + 1) * 1024]))
                            b_x1[ti].add_u(tokb16)
                            b_x1b.set_w(tokb16)
                            tokwb = E.op(s_spw[ti], b_x1b.rd(),
                                         lambda q, tt=tt, j=j, hv=hv: q.dma_start(
                                             out=d_spill_bf[j * 512 + tt * 128:j * 512 + (tt + 1) * 128,
                                                            hv * 1024:(hv + 1) * 1024],
                                             in_=x1bf[:]))
                            b_x1b.add_u(tokwb)

        def alloc_C():
            st = {}
            es = ExitStack()
            st["_es"] = es
            st["acc"] = es.enter_context(SB("acc", [128, 8 * 2048], F32))
            st["x1ch"] = es.enter_context(SB("x1ch", [128, 16 * 1024], BF16))
            st["hh"] = es.enter_context(SB("hh", [128, 16 * 1024], BF16))
            st["wgf"] = [es.enter_context(SB(f"wgf{i}", [128, 2048], F32)) for i in range(2)]
            st["wuf"] = [es.enter_context(SB(f"wuf{i}", [128, 2048], F32)) for i in range(2)]
            st["wgb"] = [es.enter_context(SB(f"wgb{i}", [128, 2048], BF16)) for i in range(2)]
            st["wub"] = [es.enter_context(SB(f"wub{i}", [128, 2048], BF16)) for i in range(2)]
            st["wd4"] = [es.enter_context(SB(f"wd4_{i}", [128, 1024], F32)) for i in range(3)]
            st["wd4b"] = [es.enter_context(SB(f"wd4b{i}", [128, 1024], BF16)) for i in range(3)]
            st["sgt"] = [es.enter_context(SB(f"sgt{i}", [128, 512], BF16)) for i in range(2)]
            st["wrb"] = es.enter_context(SB("wrb", [128, 128], BF16))
            st["wbuf"] = es.enter_context(SB("wbuf", [128, 64], F32))
            st["rsm"] = es.enter_context(SB("rsm", [128, 16], F32))
            st["prb"] = es.enter_context(SB("prb", [128, 8], F32))
            st["bnst"] = es.enter_context(SB("bnstC", [128, 4, 6], F32))
            st["bnag"] = es.enter_context(SB("bnagC", [128, 2], F32))
            return st

        def walk_C(st):
            PB = [Buf() for _ in range(8)]
            acc, x1ch, hh = st["acc"], st["x1ch"], st["hh"]
            wgf, wuf, wgb, wub = st["wgf"], st["wuf"], st["wgb"], st["wub"]
            wd4, wd4b = st["wd4"], st["wd4b"]
            sgt, wrb, wbuf, rsm = st["sgt"], st["wrb"], st["wbuf"], st["rsm"]
            prb = st["prb"]
            bnst, bnag = st["bnst"], st["bnag"]
            b_acc = Buf()
            b_x1ch = Buf()
            b_hh = Buf()
            b_wgf = [Buf(), Buf()]
            b_wuf = [Buf(), Buf()]
            b_wgb = [Buf(), Buf()]
            b_wub = [Buf(), Buf()]
            b_wd4 = [Buf(), Buf(), Buf()]
            b_wd4b = [Buf(), Buf(), Buf()]
            b_sgt = [Buf(), Buf()]
            b_wrb = Buf()
            b_wbuf = Buf()
            b_rsm = Buf()
            b_prb = Buf()
            b_bn = Buf()

            quiesce()
            tok = E.op(s_cst, [], lambda q: q.dma_start(out=wgf[0][:, 0:128], in_=d_wr))
            b_wgf[0].set_w(tok)
            tok2 = E.op(DVE, b_wgf[0].rd(), lambda q: q.tensor_copy(wrb[:], wgf[0][:, 0:128]))
            b_wgf[0].add_u(tok2)
            b_wrb.set_w(tok2)

            spw_done = [(s_spw[0], s_spw[0].n), (s_spw[1], s_spw[1].n)]
            for H in range(2):
                trt = []
                tr_tt = [[] for _ in range(8)]
                for th in range(16):
                    tt, hv = th // 2, th % 2
                    di = th % 3
                    rows = slice(H * 1024 + tt * 128, H * 1024 + (tt + 1) * 128)
                    tokA = E.op(s_spr[th % 2] if hv == 0 else s_tr[th % 2],
                                list(spw_done) + b_wd4b[di].wr(),
                                lambda q, rows=rows, di=di, hv=hv: q.dma_start(
                                    out=wd4b[di][:],
                                    in_=d_spill_bf[rows, hv * 1024:(hv + 1) * 1024]))
                    b_wd4b[di].set_w(tokA)
                    for k in range(8):
                        ct = hv * 8 + k
                        tok3 = E.op(s_tr[th % 2], [tokA] + (b_x1ch.wr() if th == 0 and k == 0 else []),
                                    lambda q, tt=tt, ct=ct, k=k, di=di: q.dma_start_transpose(
                                        x1ch[:, ct * 1024 + tt * 128:ct * 1024 + (tt + 1) * 128],
                                        wd4b[di][:, k * 128:(k + 1) * 128]))
                        b_wd4b[di].add_u(tok3)
                        tr_tt[tt].append(tok3)
                        trt.append(tok3)
                b_x1ch.set_w(trt[-1])

                ai_toks = []
                for tt in range(8):
                    w = list(spw_done) + (b_acc.wr() if tt < 2 else [])
                    tok = E.op(s_ai[tt % 2], w,
                               lambda q, tt=tt, H=H: q.dma_start(
                                   out=acc[:, tt * 2048:(tt + 1) * 2048],
                                   in_=d_spill[H * 1024 + tt * 128:H * 1024 + (tt + 1) * 128, :]))
                    ai_toks.append(tok)
                b_acc.set_w(ai_toks[-1])
                acc_init = ai_toks

                def router(tt):
                    pb = PB[4]
                    last = None
                    for ct in range(16):
                        w = (pb.wr() + b_wrb.rd() + tr_tt[tt]) if ct == 0 else []
                        last = E.op(PE, w, lambda q, tt=tt, ct=ct: q.matmul(
                            S[0][:, 0:8], x1ch[:, ct * 1024 + tt * 128:ct * 1024 + (tt + 1) * 128],
                            wrb[:, ct * 8:(ct + 1) * 8],
                            start=(ct == 0), stop=(ct == 15)), inc=(1 if ct == 15 else 0))
                    pb.set_w(last)
                    b_wrb.add_u(last)
                    b_x1ch.add_u(last)
                    ng = rsm[:, 0:1]
                    zz = rsm[:, 1:2]
                    m1 = rsm[:, 2:3]
                    m2 = rsm[:, 3:4]
                    rw = rsm[:, 4:5]
                    t1 = E.op(DVE, pb.rd() + b_rsm.wr(), lambda q, ng=ng: q.reduce_max(
                        ng, S[0][:, 0:8], axis=AX.X, negate=True))
                    t2 = E.op(ACT, [t1] + pb.rd() + b_prb.wr(), lambda q, ng=ng, zz=zz: q.activation(
                        prb[:], S[0][:, 0:8], AF.Exp, bias=ng, scale=1.0, accum_out=zz))
                    pb.add_u(t2)
                    b_prb.set_w(t2)
                    t3 = E.op(DVE, [t2], lambda q, zz=zz: q.reciprocal(zz, zz))
                    t4 = E.op(DVE, [t3] + b_prb.rd(), lambda q, zz=zz: q.tensor_scalar_mul(
                        prb[:], prb[:], zz))
                    b_prb.set_w(t4)
                    t5 = E.op(DVE, [t4], lambda q, m1=m1: q.reduce_max(m1, prb[:], axis=AX.X))
                    q2 = rsm[:, 5:13]
                    t6 = E.op(DVE, [t5], lambda q, m1=m1, q2=q2: q.tensor_scalar(
                        q2, prb[:], m1, None, ALU.is_ge))
                    t7 = E.op(DVE, [t6], lambda q, q2=q2: q.scalar_tensor_tensor(
                        q2, q2, -2.0, prb[:], ALU.mult, ALU.add))
                    t8 = E.op(DVE, [t7], lambda q, m2=m2, q2=q2: q.reduce_max(m2, q2, axis=AX.X))
                    t9 = E.op(DVE, [t8], lambda q, m1=m1, m2=m2, rw=rw: q.tensor_add(rw, m1, m2))
                    t10 = E.op(DVE, [t9], lambda q, rw=rw: q.reciprocal(rw, rw))
                    t11 = E.op(DVE, [t8], lambda q, q2=q2, m2=m2: q.tensor_scalar(
                        q2, prb[:], m2, None, ALU.is_ge))
                    t12 = E.op(DVE, [t10, t11], lambda q, q2=q2, rw=rw: q.tensor_scalar_mul(
                        q2, q2, rw))
                    t13 = E.op(DVE, [t12] + b_wbuf.wr() if tt == 0 else [t12],
                               lambda q, tt=tt, q2=q2: q.tensor_mul(
                                   wbuf[:, tt * 8:(tt + 1) * 8], q2, prb[:]))
                    b_prb.add_u(t13)
                    b_rsm.set_w(t13)
                    if tt == 7:
                        b_wbuf.set_w(t13)

                router(0)
                router(1)

                x1g = [tr_tt[0] + tr_tt[1] + tr_tt[2] + tr_tt[3],
                       tr_tt[4] + tr_tt[5] + tr_tt[6] + tr_tt[7]]
                pf = {}

                def prefetch_gu(e, ft):
                    gi = ft % 2
                    tokA = E.op(s_wa[gi], b_wgf[gi].wr(),
                                lambda q, e=e, ft=ft, gi=gi: q.dma_start(
                                    out=wgf[gi][:, 0:1024], in_=d_wg[e, ft, :, 0:1024]))
                    tokB = E.op(s_wg2[gi], b_wgf[gi].wr(),
                                lambda q, e=e, ft=ft, gi=gi: q.dma_start(
                                    out=wgf[gi][:, 1024:2048], in_=d_wg[e, ft, :, 1024:2048]))
                    b_wgf[gi].set_w(tokA)
                    b_wgf[gi].add_u(tokB)
                    tokc = E.op(DVE, [tokA, tokB] + b_wgb[gi].wr(),
                                lambda q, gi=gi: q.tensor_copy(wgb[gi][:], wgf[gi][:]))
                    b_wgf[gi].add_u(tokc)
                    b_wgb[gi].set_w(tokc)
                    tokuA = E.op(s_wb[gi], b_wuf[gi].wr(),
                                 lambda q, e=e, ft=ft, gi=gi: q.dma_start(
                                     out=wuf[gi][:, 0:1024], in_=d_wu[e, ft, :, 0:1024]))
                    tokuB = E.op(s_wu2[gi], b_wuf[gi].wr(),
                                 lambda q, e=e, ft=ft, gi=gi: q.dma_start(
                                     out=wuf[gi][:, 1024:2048], in_=d_wu[e, ft, :, 1024:2048]))
                    b_wuf[gi].set_w(tokuA)
                    b_wuf[gi].add_u(tokuB)
                    tokuc = E.op(ACT, [tokuA, tokuB] + b_wub[gi].wr(),
                                 lambda q, gi=gi: q.copy(wub[gi][:], wuf[gi][:]))
                    b_wuf[gi].add_u(tokuc)
                    b_wub[gi].set_w(tokuc)
                    pf[(e, ft)] = (tokc, tokuc)

                def compute_gu(e, ft):
                    gi = ft % 2
                    tokc, tokuc = pf.pop((e, ft))
                    tokh = None
                    for tc in range(2):
                        pbg, pbu = PB[0], PB[1]
                        lg = lu = None
                        for ct in range(16):
                            wg_ = ([tokc] + pbg.wr() + (x1g[tc] if e == 0 else []) +
                                   (b_hh.wr() if (ft == 0 and tc == 0) else [])) if ct == 0 else []
                            lg = E.op(PE, wg_, lambda q, ct=ct, tc=tc, gi=gi: q.matmul(
                                bank(0), wgb[gi][:, ct * 128:(ct + 1) * 128],
                                x1ch[:, ct * 1024 + tc * 512:ct * 1024 + (tc + 1) * 512],
                                start=(ct == 0), stop=(ct == 15)), inc=(1 if ct == 15 else 0))
                        pbg.set_w(lg)
                        b_wgb[gi].add_u(lg)
                        b_x1ch.add_u(lg)
                        for ct in range(16):
                            wu_ = ([tokuc] + pbu.wr()) if ct == 0 else []
                            lu = E.op(PE, wu_, lambda q, ct=ct, tc=tc, gi=gi: q.matmul(
                                bank(1), wub[gi][:, ct * 128:(ct + 1) * 128],
                                x1ch[:, ct * 1024 + tc * 512:ct * 1024 + (tc + 1) * 512],
                                start=(ct == 0), stop=(ct == 15)), inc=(1 if ct == 15 else 0))
                        pbu.set_w(lu)
                        b_wub[gi].add_u(lu)
                        b_x1ch.add_u(lu)
                        si = tc
                        toksl = E.op(ACT, pbg.rd() + b_sgt[si].wr(),
                                     lambda q, si=si: q.activation(sgt[si][:], bank(0), AF.Silu))
                        pbg.add_u(toksl)
                        b_sgt[si].set_w(toksl)
                        tokh = E.op(DVE, [toksl] + pbu.rd(),
                                    lambda q, ft=ft, tc=tc, si=si: q.tensor_mul(
                                        hh[:, ft * 1024 + tc * 512:ft * 1024 + (tc + 1) * 512],
                                        sgt[si][:], bank(1)))
                        pbu.add_u(tokh)
                        b_sgt[si].add_u(tokh)
                    return tokh

                dpf = {}

                def prefetch_wd(e, cn):
                    di = cn % 3
                    oc, fc = cn // 8, cn % 8
                    dtoks = []
                    for hv in range(2):
                        tok = E.op(s_wd[hv], b_wd4[di].wr(),
                                   lambda q, e=e, oc=oc, fc=fc, di=di, hv=hv: q.dma_start(
                                       out=wd4[di][:, hv * 512:(hv + 1) * 512],
                                       in_=d_wd[e, oc, fc * 2 + hv]))
                        dtoks.append(tok)
                    b_wd4[di].set_w(dtoks[0])
                    b_wd4[di].add_u(dtoks[1])
                    tokc = E.op(DVE, dtoks + b_wd4b[di].wr(),
                                lambda q, di=di: q.tensor_copy(wd4b[di][:], wd4[di][:]))
                    b_wd4[di].add_u(tokc)
                    b_wd4b[di].set_w(tokc)
                    dpf[(e, cn)] = tokc

                def compute_wd(e, cn):
                    di = cn % 3
                    oc, fc = cn // 8, cn % 8
                    tokc = dpf.pop((e, cn))
                    tokmm = None
                    for fi in range(2):
                        ft = fc * 2 + fi
                        for tt in range(8):
                            pb = PB[tt]
                            w = [tokc] + (pb.wr() + b_hh.rd() if ft == 0 else [])
                            tokmm = E.op(PE, w, lambda q, ft=ft, tt=tt, di=di, fi=fi: q.matmul(
                                bank(tt), hh[:, ft * 1024 + tt * 128:ft * 1024 + (tt + 1) * 128],
                                wd4b[di][:, fi * 512:(fi + 1) * 512],
                                start=(ft == 0), stop=(ft == 15)),
                                inc=(1 if tt == 7 else 0))
                        b_hh.add_u(tokmm)
                    b_wd4b[di].add_u(tokmm)
                    if fc == 7:
                        for tt in range(8):
                            pb = PB[tt]
                            pb.set_w(tokmm)
                            wcol = wbuf[:, tt * 8 + e:tt * 8 + e + 1]
                            tok = E.op(DVE, pb.rd() + b_wbuf.rd() + b_acc.rd() + acc_init,
                                       lambda q, tt=tt, oc=oc, wcol=wcol: q.scalar_tensor_tensor(
                                           acc[:, tt * 2048 + oc * 512:tt * 2048 + (oc + 1) * 512],
                                           bank(tt), wcol,
                                           acc[:, tt * 2048 + oc * 512:tt * 2048 + (oc + 1) * 512],
                                           ALU.mult, ALU.add))
                            pb.add_u(tok)
                            b_wbuf.add_u(tok)
                            b_acc.add_u(tok)

                prefetch_gu(0, 0)
                for e in range(NE):
                    for ft in range(16):
                        if ft + 1 < 16:
                            prefetch_gu(e, ft + 1)
                        elif e + 1 < NE:
                            prefetch_gu(e + 1, 0)
                        if ft == 14:
                            prefetch_wd(e, 0)
                        elif ft == 15:
                            prefetch_wd(e, 1)
                        if e == 0 and ft in (2, 4, 6, 8, 10, 12):
                            router(ft // 2 + 1)
                        tokh = compute_gu(e, ft)
                    b_hh.set_w(tokh)
                    for cn in range(32):
                        if cn + 2 < 32:
                            prefetch_wd(e, cn + 2)
                        compute_wd(e, cn)

                acc_users = list(b_acc.u)
                for tt in range(8):
                    arow = acc[:, tt * 2048:(tt + 1) * 2048]
                    for ch in range(4):
                        tokb = E.op(DVE, (acc_users if ch == 0 else []) + (b_bn.wr() if ch == 0 else []),
                                    lambda q, tt=tt, ch=ch, arow=arow: q.bn_stats(
                                        bnst[:, ch, :], arow[:, ch * 512:(ch + 1) * 512]))
                    tokb = E.op(DVE, [tokb], lambda q: q.bn_aggr(bnag[:], bnst[:]))
                    b_bn.set_w(tokb)
                    ve = rsm[:, 13:14]
                    nm = rsm[:, 14:15]
                    t1 = E.op(DVE, [tokb] + b_rsm.wr(), lambda q, ve=ve: q.tensor_scalar_add(
                        ve, bnag[:, 1:2], EPS))
                    t2 = E.op(ACT, [t1], lambda q, ve=ve: q.activation(ve, ve, AF.Ln))
                    t3 = E.op(DVE, [t2], lambda q, ve=ve: q.tensor_scalar_mul(ve, ve, -0.5))
                    t4 = E.op(ACT, [t3], lambda q, ve=ve: q.activation(ve, ve, AF.Exp))
                    t5 = E.op(DVE, [tokb], lambda q, nm=nm: q.tensor_scalar_mul(nm, bnag[:, 0:1], -1.0))
                    b_rsm.set_w(t5)
                    tok = E.op(DVE, [t4, t5], lambda q, arow=arow, nm=nm, ve=ve: q.tensor_scalar(
                        arow, arow, nm, ve, ALU.add, ALU.mult))
                    b_bn.add_u(tok)
                    tokd = E.op(s_ob[tt % 2], [tok], lambda q, tt=tt, H=H, arow=arow: q.dma_start(
                        out=d_out[H * 1024 + tt * 128:H * 1024 + (tt + 1) * 128, :], in_=arow))
                    b_acc.add_u(tokd)

            E.wait_only(s_ob[0], [(s_ob[0], s_ob[0].n), (s_ob[1], s_ob[1].n)])

        with nc.Block() as blkA:
            @blkA.tensor
            def _(q):
                E.begin("PE", q); walk_A()
            @blkA.vector
            def _(q):
                E.begin("DVE", q); walk_A()
            @blkA.scalar
            def _(q):
                E.begin("ACT", q); walk_A()
            @blkA.sync
            def _(q):
                E.begin("SP", q); walk_A()
            @blkA.gpsimd
            def _(q):
                E.begin("PL", q); walk_A()

        esA.close()
        totals_A = [eq.n for eq in E.eqs]

        def resume(mode, q, totals):
            E.mode = mode
            E.q = q
            for eq, n in zip(E.eqs, totals):
                eq.n = n
                eq.waited = {}

        stB = alloc_B()
        with stB["_es"]:
            with nc.Block() as blkB:
                @blkB.tensor
                def _(q):
                    resume("PE", q, totals_A); walk_B(stB)
                @blkB.vector
                def _(q):
                    resume("DVE", q, totals_A); walk_B(stB)
                @blkB.scalar
                def _(q):
                    resume("ACT", q, totals_A); walk_B(stB)
                @blkB.sync
                def _(q):
                    resume("SP", q, totals_A); walk_B(stB)
                @blkB.gpsimd
                def _(q):
                    resume("PL", q, totals_A); walk_B(stB)
        totals_B = [eq.n for eq in E.eqs]
        esAB.close()

        stC = alloc_C()
        with stC["_es"]:
            with nc.Block() as blkC:
                @blkC.tensor
                def _(q):
                    resume("PE", q, totals_B); walk_C(stC)
                @blkC.vector
                def _(q):
                    resume("DVE", q, totals_B); walk_C(stC)
                @blkC.scalar
                def _(q):
                    resume("ACT", q, totals_B); walk_C(stC)
                @blkC.sync
                def _(q):
                    resume("SP", q, totals_B); walk_C(stC)
                @blkC.gpsimd
                def _(q):
                    resume("PL", q, totals_B); walk_C(stC)
    return nc


def _tables():
    inv = (10000.0 ** (-np.arange(0, HD, 2, dtype=np.float64) / HD)).astype(np.float64)
    tpos = np.arange(T, dtype=np.float64)
    fr = np.outer(tpos, inv)
    emb = np.concatenate([fr, fr], axis=1)
    cos = np.cos(emb).T.astype(BFD)
    sin = np.sin(emb).T.astype(BFD)
    return np.ascontiguousarray(cos), np.ascontiguousarray(sin)


def _masks(half):
    cs = CHUNK_SETS[half]
    m2 = np.zeros((NSLOT, 8, 128, 512), dtype=np.float32)
    for j in range(NSLOT):
        g = cs[j]
        n = NK[j]
        for kc2 in range(8):
            kabs = (4 * (n - 2)) + kc2
            kg = kabs * 128 + np.arange(128)[:, None]
            qg = g * 512 + np.arange(512)[None, :]
            m2[j, kc2] = np.where(kg <= qg, 1.0, 0.0)
    return m2.astype(BFD)


_NC_CACHE = {}


def kernel(L0_attn_x, L0_attn_Wq, L0_attn_Wk, L0_attn_Wv, L0_attn_Wproj,
           L0_ffn_Wr, L0_ffn_Wg, L0_ffn_Wu, L0_ffn_Wd):
    x = np.ascontiguousarray(np.asarray(L0_attn_x, dtype=np.float32))
    Wq = np.asarray(L0_attn_Wq, dtype=np.float32)
    Wk = np.asarray(L0_attn_Wk, dtype=np.float32)
    Wv = np.asarray(L0_attn_Wv, dtype=np.float32)
    Wp = np.asarray(L0_attn_Wproj, dtype=np.float32)
    Wr = np.asarray(L0_ffn_Wr, dtype=np.float32)
    Wg = np.asarray(L0_ffn_Wg, dtype=np.float32)
    Wu = np.asarray(L0_ffn_Wu, dtype=np.float32)
    Wd = np.asarray(L0_ffn_Wd, dtype=np.float32)

    cos, sin = _tables()
    wq_t = np.ascontiguousarray(
        Wq.reshape(16, 128, 16, 128).transpose(2, 1, 0, 3).reshape(16, 128, 2048))
    wk_t = np.ascontiguousarray(
        Wk.reshape(16, 128, 4, 128).transpose(1, 2, 0, 3).reshape(128, 8192))
    wv_t = np.ascontiguousarray(Wv.reshape(16, 128, 512).transpose(1, 0, 2).reshape(128, 8192))
    wp_t = np.ascontiguousarray(Wp.reshape(16, 128, 2048))
    wr_t = np.ascontiguousarray(Wr.reshape(16, 128, 8).transpose(1, 0, 2).reshape(128, 128))
    wg_t = np.ascontiguousarray(
        Wg.reshape(NE, 16, 128, 16, 128).transpose(0, 3, 2, 1, 4).reshape(NE, 16, 128, 2048))
    wu_t = np.ascontiguousarray(
        Wu.reshape(NE, 16, 128, 16, 128).transpose(0, 3, 2, 1, 4).reshape(NE, 16, 128, 2048))
    wd_t = np.ascontiguousarray(
        Wd.reshape(NE, 16, 128, 4, 512).transpose(0, 3, 1, 2, 4))
    ones_f = np.ones((1, 128), dtype=np.float32)
    ones_c = np.ones((128, 1), dtype=BFD)
    konst = np.stack([np.full(128, -SHIFT, np.float32), np.full(128, SHIFT, np.float32)], axis=1)

    in_maps = []
    row_sets = []
    for c in range(NCORE):
        b, half = c // 2, c % 2
        cs = CHUNK_SETS[half]
        rows = np.concatenate([np.arange(g * 512, (g + 1) * 512) for g in cs])
        row_sets.append((b, rows))
        xb = x[b]
        xT = np.ascontiguousarray(xb.T)
        m2 = _masks(half)
        in_maps.append({
            "xT": xT,
            "xqT": np.ascontiguousarray(xT[:, rows]),
            "xrows": np.ascontiguousarray(xb[rows]),
            "cosk": cos, "sink": sin,
            "cosq": np.ascontiguousarray(cos[:, rows]),
            "sinq": np.ascontiguousarray(sin[:, rows]),
            "maskp2": m2,
            "wq_t": wq_t, "wk_t": wk_t, "wv_t": wv_t, "wp_t": wp_t, "wr_t": wr_t,
            "wg_t": wg_t, "wu_t": wu_t, "wd_t": wd_t,
            "ones_f": ones_f, "ones_c": ones_c, "konst": konst,
        })

    if "nc" not in _NC_CACHE:
        _NC_CACHE["nc"] = build_nc()
    nc = _NC_CACHE["nc"]
    res = run_bass_kernel_spmd(nc, in_maps, core_ids=list(range(NCORE)))
    out = np.empty((B, T, C), dtype=np.float32)
    for c in range(NCORE):
        b, rows = row_sets[c]
        out[b][rows] = np.asarray(res.results[c]["out_rows"], dtype=np.float32)
    return out


# version 12
# speedup vs baseline: 1.1345x; 1.1345x over previous
import numpy as np
import ml_dtypes
import concourse.bass as bass
import concourse.mybir as mybir
from concourse.bass_utils import run_bass_kernel_spmd
from contextlib import ExitStack

F32 = mybir.dt.float32
BF16 = mybir.dt.bfloat16
AF = mybir.ActivationFunctionType
ALU = mybir.AluOpType
AX = mybir.AxisListType

B, T, C = 4, 4096, 2048
NH, NKV, HD = 16, 4, 128
NE = 8
NCORE = 8
SCALE = 1.0 / float(np.sqrt(HD))
SHIFT = 12.0
EPS = 1e-5
CHUNK_SETS = [[7, 4, 2, 1], [6, 5, 3, 0]]
NK = [8, 6, 4, 2]
NSLOT = 4
QR = 2048
BFD = ml_dtypes.bfloat16


class EQ:
    def __init__(self, queue_name, step=1):
        self.queue = queue_name
        self.step = step
        self.sem = None
        self.n = 0
        self.waited = {}


class Emitter:
    def __init__(self):
        self.eqs = []
        self.mode = None
        self.q = None

    def new_eq(self, queue_name, step=1):
        eq = EQ(queue_name, step)
        self.eqs.append(eq)
        return eq

    def begin(self, mode, q, reset=True):
        self.mode = mode
        self.q = q
        if reset:
            for eq in self.eqs:
                eq.n = 0
                eq.waited = {}

    def _wait(self, eq, waits):
        agg = {}
        for t in waits:
            if t is None:
                continue
            weq, val = t
            if val <= 0:
                continue
            k = id(weq)
            if k not in agg or agg[k][1] < val:
                agg[k] = (weq, val)
        for k, (weq, val) in agg.items():
            if eq.waited.get(k, 0) >= val:
                continue
            self.q.wait_ge(weq.sem, val)
            eq.waited[k] = val

    def op(self, eq, waits, fn, inc=None):
        inc = eq.step if inc is None else inc
        if self.mode == eq.queue:
            self._wait(eq, waits)
            ins = fn(self.q)
            if inc:
                ins.then_inc(eq.sem, inc)
        eq.n += inc
        return (eq, eq.n)

    def wait_only(self, eq, waits):
        if self.mode == eq.queue:
            self._wait(eq, waits)


class Buf:
    def __init__(self):
        self.r = None
        self.u = []

    def wr(self):
        return [t for t in [self.r] + self.u if t is not None]

    def rd(self):
        return [self.r] if self.r is not None else []

    def set_w(self, tok):
        self.r = tok
        self.u = []

    def add_u(self, tok):
        self.u.append(tok)


def build_nc():
    nc = bass.Bass("TRN2", target_bir_lowering=False, debug=False)

    d_xT = nc.dram_tensor("xT", [C, T], F32, kind="ExternalInput").ap()
    d_xqT = nc.dram_tensor("xqT", [C, QR], F32, kind="ExternalInput").ap()
    d_xrows = nc.dram_tensor("xrows", [QR, C], F32, kind="ExternalInput").ap()
    d_cosk = nc.dram_tensor("cosk", [128, T], BF16, kind="ExternalInput").ap()
    d_sink = nc.dram_tensor("sink", [128, T], BF16, kind="ExternalInput").ap()
    d_cosq = nc.dram_tensor("cosq", [128, QR], BF16, kind="ExternalInput").ap()
    d_sinq = nc.dram_tensor("sinq", [128, QR], BF16, kind="ExternalInput").ap()
    d_mk2 = nc.dram_tensor("maskp2", [NSLOT, 8, 128, 512], BF16, kind="ExternalInput").ap()
    d_onec = nc.dram_tensor("ones_c", [128, 1], BF16, kind="ExternalInput").ap()
    d_wq = nc.dram_tensor("wq_t", [16, 128, 2048], F32, kind="ExternalInput").ap()
    d_wk = nc.dram_tensor("wk_t", [128, 8192], F32, kind="ExternalInput").ap()
    d_wv = nc.dram_tensor("wv_t", [128, 8192], F32, kind="ExternalInput").ap()
    d_wp = nc.dram_tensor("wp_t", [16, 128, 2048], F32, kind="ExternalInput").ap()
    d_wr = nc.dram_tensor("wr_t", [128, 128], F32, kind="ExternalInput").ap()
    d_wg = nc.dram_tensor("wg_t", [NE, 16, 128, 2048], F32, kind="ExternalInput").ap()
    d_wu = nc.dram_tensor("wu_t", [NE, 16, 128, 2048], F32, kind="ExternalInput").ap()
    d_wd = nc.dram_tensor("wd_t", [NE, 4, 16, 128, 512], F32, kind="ExternalInput").ap()
    d_ones = nc.dram_tensor("ones_f", [1, 128], F32, kind="ExternalInput").ap()
    d_konst = nc.dram_tensor("konst", [128, 2], F32, kind="ExternalInput").ap()
    d_out = nc.dram_tensor("out_rows", [QR, C], F32, kind="ExternalOutput").ap()
    d_spill = nc.dram_tensor("spill", [QR, C], F32, kind="Internal").ap()
    d_spill_bf = nc.dram_tensor("spill_bf", [QR, C], BF16, kind="Internal").ap()
    d_wqbf = nc.dram_tensor("wq_bf", [16, 128, 2048], BF16, kind="Internal").ap()
    d_wpbf = nc.dram_tensor("wp_bf", [16, 128, 2048], BF16, kind="Internal").ap()

    E = Emitter()
    PE = E.new_eq("PE")
    DVE = E.new_eq("DVE")
    ACT = E.new_eq("ACT")
    s_cst = E.new_eq("PL", 16)
    s_xa = [E.new_eq("SP", 16), E.new_eq("PL", 16), E.new_eq("SP", 16), E.new_eq("PL", 16)]
    s_mk = E.new_eq("PL", 16)
    s_xr = [E.new_eq("PL", 16) for _ in range(2)]
    s_spw = [E.new_eq("PL", 16) for _ in range(2)]
    s_spr = [E.new_eq("PL", 16) for _ in range(2)]
    s_ai = [E.new_eq("PL", 16) for _ in range(2)]
    s_ob = [E.new_eq("PL", 16) for _ in range(2)]
    s_wa = [E.new_eq("SP", 16) for _ in range(2)]
    s_tr = [E.new_eq("SP", 16) for _ in range(2)]
    s_wb = [E.new_eq("ACT", 16) for _ in range(2)]
    s_wg2 = [E.new_eq("PL", 16) for _ in range(2)]
    s_wu2 = [E.new_eq("PL", 16) for _ in range(2)]
    s_wd = [E.new_eq("SP", 16), E.new_eq("PL", 16)]

    top = ExitStack()
    for i, eq in enumerate(E.eqs):
        eq.sem = top.enter_context(nc.semaphore(f"sem{i}"))

    SB = lambda name, shape, dt: nc.sbuf_tensor(name, shape, dt)

    with top:
        ones_sb = top.enter_context(SB("ones_sb", [1, 128], F32))
        onec_sb = top.enter_context(SB("onec_sb", [128, 1], BF16))
        ksb = top.enter_context(SB("ksb", [128, 2], F32))
        esAB = ExitStack()
        kch = esAB.enter_context(SB("kch", [128, 4 * T], BF16))
        vtm = esAB.enter_context(SB("vtm", [128, 32 * 512], BF16))
        xstag = [esAB.enter_context(SB(f"xstag{i}", [128, 512], F32)) for i in range(2)]
        xch = esAB.enter_context(SB("xch", [128, 16 * 512], BF16))
        ropet = [esAB.enter_context(SB(f"ropet{i}", [128, 512], F32)) for i in range(2)]
        W0 = top.enter_context(nc.psum_tensor("W0", [128, 1024], F32))
        W1 = top.enter_context(nc.psum_tensor("W1", [128, 1024], F32))
        S = [top.enter_context(nc.psum_tensor(f"S{i}", [128, 512], F32)) for i in range(4)]

        def bank(i):
            if i == 0:
                return W0[:, 0:512]
            if i == 1:
                return W0[:, 512:1024]
            if i == 2:
                return W1[:, 0:512]
            if i == 3:
                return W1[:, 512:1024]
            return S[i - 4][:]

        def quiesce():
            toks = []
            for s in ([s_cst, s_mk] + s_xa + s_xr + s_spw + s_spr + s_ai + s_wa + s_tr
                      + s_wb + s_wg2 + s_wu2 + s_wd + s_ob):
                toks.append((s, s.n))
            for e in (PE, DVE, ACT):
                E.wait_only(e, toks)
            E.wait_only(s_wa[0], toks)
            E.wait_only(s_cst, toks)

        def mk_rope(PB, b_rt, b_tab):
            def rope(bi, cos_ap, sin_ap, out_ap, out_waits=()):
                pb = PB[bi]
                bk = bank(bi)
                t1, t2 = ropet
                bt1, bt2 = b_rt
                tok = E.op(DVE, pb.rd() + bt1.wr() + b_tab.rd(),
                           lambda q: q.tensor_mul(t1[:], bk, cos_ap))
                bt1.set_w(tok)
                b_tab.add_u(tok)
                E.op(DVE, pb.rd() + bt2.wr(),
                     lambda q: q.tensor_mul(t2[0:64, :], bk[64:128, :], sin_ap[64:128]))
                tok2 = E.op(DVE, pb.rd(),
                            lambda q: q.tensor_mul(t2[64:128, :], bk[0:64, :], sin_ap[0:64]))
                bt2.set_w(tok2)
                pb.add_u(tok2)
                b_tab.add_u(tok2)
                wa = bt1.rd() + bt2.rd() + list(out_waits)
                E.op(DVE, wa, lambda q: q.tensor_sub(out_ap[0:64], t1[0:64, :], t2[0:64, :]))
                tb = E.op(DVE, wa, lambda q: q.tensor_add(out_ap[64:128], t1[64:128, :], t2[64:128, :]))
                bt1.add_u(tb)
                bt2.add_u(tb)
                return tb
            return rope

        stA = {}
        esA = ExitStack()
        stA["tabk_c"] = esA.enter_context(SB("tabk_c", [128, T], BF16))
        stA["tabk_s"] = esA.enter_context(SB("tabk_s", [128, T], BF16))
        stA["wkb"] = esA.enter_context(SB("wkb", [128, 8192], BF16))
        stA["wvb"] = esA.enter_context(SB("wvb", [128, 8192], BF16))
        stA["wkstag"] = [esA.enter_context(SB(f"wkstag{i}", [128, 4096], F32)) for i in range(2)]

        def walk_A():
            PB = [Buf() for _ in range(8)]
            b_cst = Buf()
            b_stag = [Buf() for _ in range(4)]
            b_xch = Buf()
            b_rt = [Buf(), Buf()]
            b_tab = Buf()
            b_wkb, b_wvb = Buf(), Buf()
            b_ws = [Buf(), Buf()]
            tabk_c, tabk_s = stA["tabk_c"], stA["tabk_s"]
            wkb, wvb, wkstag = stA["wkb"], stA["wvb"], stA["wkstag"]

            for dst, src in [(tabk_c[:], d_cosk), (tabk_s[:], d_sink),
                             (ones_sb[:], d_ones),
                             (onec_sb[:], d_onec), (ksb[:], d_konst)]:
                E.op(s_cst, [], lambda q, dst=dst, src=src: q.dma_start(out=dst, in_=src))
            E.op(s_cst, [], lambda q: q.dma_start(out=wkstag[0][:], in_=d_wk[:, 0:4096]))
            tok = E.op(s_cst, [], lambda q: q.dma_start(out=wkstag[1][:], in_=d_wk[:, 4096:8192]))
            b_cst.set_w(tok)
            b_tab.set_w(tok)
            E.op(DVE, b_cst.rd(), lambda q: q.tensor_copy(wkb[:, 0:4096], wkstag[0][:]))
            t = E.op(DVE, b_cst.rd(), lambda q: q.tensor_copy(wkb[:, 4096:8192], wkstag[1][:]))
            b_wkb.set_w(t)
            b_ws[0].set_w(t)
            b_ws[1].set_w(t)
            E.op(s_cst, b_ws[0].wr(), lambda q: q.dma_start(out=wkstag[0][:], in_=d_wv[:, 0:4096]))
            tok = E.op(s_cst, b_ws[1].wr(), lambda q: q.dma_start(out=wkstag[1][:], in_=d_wv[:, 4096:8192]))
            b_ws[0].set_w(tok)
            b_ws[1].set_w(tok)
            tv0 = E.op(DVE, b_ws[0].rd(), lambda q: q.tensor_copy(wvb[:, 0:4096], wkstag[0][:]))
            t = E.op(DVE, b_ws[1].rd(), lambda q: q.tensor_copy(wvb[:, 4096:8192], wkstag[1][:]))
            b_wvb.set_w(t)
            stag4 = [xstag[0][:], xstag[1][:], wkstag[0][:, 0:512], wkstag[1][:, 0:512]]
            b_stag[2].set_w(tv0)
            b_stag[3].set_w(t)

            rope = mk_rope(PB, b_rt, b_tab)

            for tc in range(8):
                conv = []
                for ct in range(16):
                    si = ct % 4
                    tok = E.op(s_xa[si], b_stag[si].wr(),
                               lambda q, ct=ct, tc=tc, si=si: q.dma_start(
                                   out=stag4[si],
                                   in_=d_xT[ct * 128:(ct + 1) * 128, tc * 512:(tc + 1) * 512]))
                    b_stag[si].set_w(tok)
                    w = b_stag[si].rd() + (b_xch.wr() if ct == 0 else [])
                    tok2 = E.op(DVE, w, lambda q, ct=ct, si=si: q.tensor_copy(
                        xch[:, ct * 512:(ct + 1) * 512], stag4[si]))
                    b_stag[si].add_u(tok2)
                    conv.append(tok2)
                b_xch.set_w(conv[-1])

                for g in range(4):
                    bi = g % 2
                    pb = PB[bi]
                    last = None
                    for ct in range(16):
                        w = (pb.wr() + b_wkb.rd() + [conv[15]]) if ct == 0 else []
                        last = E.op(PE, w, lambda q, g=g, ct=ct, bi=bi: q.matmul(
                            bank(bi), wkb[:, (g * 16 + ct) * 128:(g * 16 + ct + 1) * 128],
                            xch[:, ct * 512:(ct + 1) * 512],
                            start=(ct == 0), stop=(ct == 15)), inc=(1 if ct == 15 else 0))
                    pb.set_w(last)
                    b_wkb.add_u(last)
                    b_xch.add_u(last)
                    rope(bi, tabk_c[:, tc * 512:(tc + 1) * 512],
                         tabk_s[:, tc * 512:(tc + 1) * 512],
                         kch[:, g * T + tc * 512:g * T + (tc + 1) * 512])

                for tt in range(4):
                    bi = 2 + tt % 2
                    pb = PB[bi]
                    last = None
                    for ct in range(16):
                        w = (pb.wr() + b_wvb.rd() + [conv[15]]) if ct == 0 else []
                        last = E.op(PE, w, lambda q, tt=tt, ct=ct, bi=bi: q.matmul(
                            bank(bi), xch[:, ct * 512 + tt * 128:ct * 512 + (tt + 1) * 128],
                            wvb[:, ct * 512:(ct + 1) * 512],
                            start=(ct == 0), stop=(ct == 15)), inc=(1 if ct == 15 else 0))
                    pb.set_w(last)
                    b_wvb.add_u(last)
                    b_xch.add_u(last)
                    tok = E.op(DVE, pb.rd(), lambda q, tt=tt, tc=tc, bi=bi: q.tensor_copy(
                        vtm[:, (tc * 4 + tt) * 512:(tc * 4 + tt + 1) * 512], bank(bi)))
                    pb.add_u(tok)

        def alloc_B():
            st = {}
            es = ExitStack()
            st["_es"] = es
            st["tabq_c"] = es.enter_context(SB("tabq_c", [128, 512], BF16))
            st["tabq_s"] = es.enter_context(SB("tabq_s", [128, 512], BF16))
            st["qch"] = es.enter_context(SB("qch", [128, 16 * 512], BF16))
            st["ptb"] = [es.enter_context(SB(f"ptb{i}", [128, 1024], BF16)) for i in range(3)]
            st["ych"] = es.enter_context(SB("ych", [128, 16 * 512], BF16))
            st["mk2"] = es.enter_context(SB("mk2", [128, 8 * 512], BF16))
            st["wstf"] = [es.enter_context(SB(f"wstf{i}", [128, 2048], F32)) for i in range(2)]
            st["wqb"] = [es.enter_context(SB(f"wqb{i}", [128, 2048], BF16)) for i in range(4)]
            st["x1pre"] = [es.enter_context(SB(f"x1pre{i}", [128, C], F32)) for i in range(2)]
            st["xres"] = [es.enter_context(SB(f"xres{i}", [128, C], F32)) for i in range(2)]
            st["x1bf"] = es.enter_context(SB("x1bf", [128, 1024], BF16))
            st["sesb"] = es.enter_context(SB("sesb", [1, 512], F32))
            st["csb"] = [es.enter_context(SB(f"csb{i}", [128, 512], BF16)) for i in range(2)]
            st["sm"] = es.enter_context(SB("sm", [128, 4], F32))
            st["bnst"] = es.enter_context(SB("bnst", [128, 4, 6], F32))
            st["bnag"] = es.enter_context(SB("bnag", [128, 2], F32))
            return st

        def walk_B(st):
            PB = [Buf() for _ in range(8)]
            tabq_c, tabq_s = st["tabq_c"], st["tabq_s"]
            qch, ych = st["qch"], st["ych"]
            ptb = st["ptb"]
            mk2 = st["mk2"]
            wstf, wqb = st["wstf"], st["wqb"]
            x1pre, xres = st["x1pre"], st["xres"]
            x1bf = st["x1bf"]
            csb, sm = st["csb"], st["sm"]
            sesb = st["sesb"]
            bnst, bnag = st["bnst"], st["bnag"]

            b_tab = Buf()
            b_xch = Buf()
            b_stag = [Buf() for _ in range(2)]
            b_qch = Buf()
            b_ptb = [Buf() for _ in range(3)]
            b_ych = Buf()
            b_mk2 = Buf()
            b_ws = [Buf() for _ in range(2)]
            b_wqb = [Buf() for _ in range(4)]
            b_rt = [Buf(), Buf()]
            b_x1 = [Buf(), Buf()]
            b_xr = [Buf(), Buf()]
            b_se = Buf()
            b_cs = [Buf(), Buf()]
            b_sm = [Buf(), Buf()]
            b_bn = Buf()
            b_x1b = Buf()
            b_kch, b_vtm = Buf(), Buf()

            quiesce()
            rope = mk_rope(PB, b_rt, b_tab)

            for j in range(NSLOT):
                n = NK[j]
                E.op(s_cst, b_tab.wr(), lambda q, j=j: q.dma_start(
                    out=tabq_c[:], in_=d_cosq[:, j * 512:(j + 1) * 512]))
                tok = E.op(s_cst, [], lambda q, j=j: q.dma_start(
                    out=tabq_s[:], in_=d_sinq[:, j * 512:(j + 1) * 512]))
                b_tab.set_w(tok)
                tokm = E.op(s_mk, b_mk2.wr(), lambda q, j=j: q.dma_start(
                    out=mk2[:].rearrange("p (a f) -> p a f", a=8),
                    in_=d_mk2[j].rearrange("a p f -> p a f")))
                b_mk2.set_w(tokm)

                conv = []
                for ct in range(16):
                    si = ct % 2
                    tok = E.op(s_xa[si], b_stag[si].wr(),
                               lambda q, ct=ct, j=j: q.dma_start(
                                   out=xstag[ct % 2][:],
                                   in_=d_xqT[ct * 128:(ct + 1) * 128, j * 512:(j + 1) * 512]))
                    b_stag[si].set_w(tok)
                    w = b_stag[si].rd() + (b_xch.wr() if ct == 0 else [])
                    tok2 = E.op(DVE, w, lambda q, ct=ct: q.tensor_copy(
                        xch[:, ct * 512:(ct + 1) * 512], xstag[ct % 2][:]))
                    b_stag[si].add_u(tok2)
                    conv.append(tok2)
                b_xch.set_w(conv[-1])

                for qc in range(16):
                    ci = qc % 4
                    if j == 0:
                        wi = qc % 2
                        tokA = E.op(s_wa[qc % 2], b_ws[wi].wr(),
                                    lambda q, qc=qc, wi=wi: q.dma_start(
                                        out=wstf[wi][:, 0:1024], in_=d_wq[qc, :, 0:1024]))
                        tokB = E.op(s_wg2[qc % 2], b_ws[wi].wr(),
                                    lambda q, qc=qc, wi=wi: q.dma_start(
                                        out=wstf[wi][:, 1024:2048], in_=d_wq[qc, :, 1024:2048]))
                        b_ws[wi].set_w(tokA)
                        b_ws[wi].add_u(tokB)
                        tok2 = E.op(DVE, [tokA, tokB] + b_wqb[ci].wr(),
                                    lambda q, wi=wi, ci=ci: q.tensor_copy(wqb[ci][:], wstf[wi][:]))
                        b_ws[wi].add_u(tok2)
                        b_wqb[ci].set_w(tok2)
                        tokwb = E.op(s_ob[qc % 2], b_wqb[ci].rd(),
                                     lambda q, qc=qc, ci=ci: q.dma_start(
                                         out=d_wqbf[qc], in_=wqb[ci][:]))
                        b_wqb[ci].add_u(tokwb)
                        ready = [tok2]
                    else:
                        tokA = E.op(s_wa[qc % 2], b_wqb[ci].wr(),
                                    lambda q, qc=qc, ci=ci: q.dma_start(
                                        out=wqb[ci][:, 0:1024], in_=d_wqbf[qc, :, 0:1024]))
                        tokB = E.op(s_wg2[qc % 2], b_wqb[ci].wr(),
                                    lambda q, qc=qc, ci=ci: q.dma_start(
                                        out=wqb[ci][:, 1024:2048], in_=d_wqbf[qc, :, 1024:2048]))
                        b_wqb[ci].set_w(tokA)
                        b_wqb[ci].add_u(tokB)
                        ready = [tokA, tokB]
                    bi = qc % 2
                    pb = PB[bi]
                    last = None
                    for ct in range(16):
                        w = (pb.wr() + ready + [conv[15]] + (b_qch.wr() if qc == 0 else [])) \
                            if ct == 0 else []
                        last = E.op(PE, w, lambda q, ct=ct, bi=bi, ci=ci: q.matmul(
                            bank(bi), wqb[ci][:, ct * 128:(ct + 1) * 128],
                            xch[:, ct * 512:(ct + 1) * 512],
                            start=(ct == 0), stop=(ct == 15)), inc=(1 if ct == 15 else 0))
                    pb.set_w(last)
                    b_wqb[ci].add_u(last)
                    b_xch.add_u(last)
                    tb = rope(bi, tabq_c[:], tabq_s[:],
                              qch[:, qc * 512:(qc + 1) * 512],
                              out_waits=(b_qch.wr() if qc == 0 else []))
                    if qc == 15:
                        b_qch.set_w(tb)

                def epi_a(h):
                    pb7 = PB[7]
                    t1 = E.op(DVE, pb7.rd() + b_se.wr(),
                              lambda q: q.tensor_copy(sesb[:], S[3][0:1, 0:512]))
                    pb7.add_u(t1)
                    b_se.set_w(t1)

                def epi_b(h):
                    par = h % 2
                    yb = 6 if par == 0 else 4
                    pb5 = PB[5]
                    t3 = E.op(PE, pb5.wr() + b_se.rd(),
                              lambda q: q.matmul(S[1][:], ones_sb[:], sesb[:],
                                                 start=True, stop=True))
                    pb5.set_w(t3)
                    b_se.add_u(t3)
                    def _recip_bf16(q, par):
                        with nc.allow_low_precision(reason="bf16 1/se factor"):
                            return q.reciprocal(csb[par][:], S[1][:])
                    t4 = E.op(DVE, pb5.rd() + b_cs[par].wr(),
                              lambda q, par=par: _recip_bf16(q, par))
                    pb5.add_u(t4)
                    b_cs[par].set_w(t4)
                    pbY = PB[yb]
                    tok = E.op(DVE, pbY.rd() + b_cs[par].rd() + (b_ych.wr() if h == 0 else []),
                               lambda q, h=h, yb=yb, par=par: q.tensor_mul(
                                   ych[:, h * 512:(h + 1) * 512], bank(yb), csb[par][:]))
                    pbY.add_u(tok)
                    b_cs[par].add_u(tok)
                    if h == 15:
                        b_ych.set_w(tok)

                def pass2(h, prev):
                    par = h % 2
                    g = h // 4
                    n4 = n * 4
                    yb = 6 if par == 0 else 4
                    P = n4 // 2
                    if prev is not None:
                        epi_a(prev)

                    def stage1(kp):
                        wi = kp % 2
                        pi = kp % 3
                        pbl, pbr = PB[2 * wi], PB[2 * wi + 1]
                        mmtoks = []
                        for half in range(2):
                            kc2 = kp * 2 + half
                            pb = pbl if half == 0 else pbr
                            tok = E.op(PE, pb.wr() + b_qch.rd(),
                                       lambda q, kc2=kc2, half=half, g=g, h=h, wi=wi: q.matmul(
                                           bank(2 * wi + half),
                                           kch[:, g * T + kc2 * 128:g * T + (kc2 + 1) * 128],
                                           qch[:, h * 512:(h + 1) * 512],
                                           start=True, stop=True))
                            pb.set_w(tok)
                            b_qch.add_u(tok)
                            b_kch.add_u(tok)
                            mmtoks.append(tok)
                        tok = E.op(ACT, mmtoks + b_ptb[pi].wr(),
                                   lambda q, wi=wi, pi=pi: q.activation(
                                       ptb[pi][:], (W0 if wi == 0 else W1)[:], AF.Exp,
                                       bias=ksb[:, 0:1], scale=SCALE))
                        pbl.add_u(tok)
                        pbr.add_u(tok)
                        b_ptb[pi].set_w(tok)
                        if kp >= P - 4:
                            mi = kp - (P - 4)
                            tok = E.op(DVE, b_ptb[pi].rd() + b_mk2.rd(),
                                       lambda q, pi=pi, mi=mi: q.tensor_mul(
                                           ptb[pi][:], ptb[pi][:],
                                           mk2[:, mi * 1024:(mi + 1) * 1024]))
                            b_ptb[pi].set_w(tok)
                            b_mk2.add_u(tok)

                    res = {}

                    def stage2(kp):
                        pi = kp % 3
                        pbY = PB[yb]
                        pb7 = PB[7]
                        for half in range(2):
                            kc2 = kp * 2 + half
                            w = b_ptb[pi].rd() + (pbY.wr() if kc2 == 0 else [])
                            ylast = E.op(PE, w, lambda q, kc2=kc2, g=g, pi=pi, half=half, yb=yb: q.matmul(
                                bank(yb), vtm[:, kc2 * 512 + g * 128:kc2 * 512 + (g + 1) * 128],
                                ptb[pi][:, half * 512:(half + 1) * 512],
                                start=(kc2 == 0), stop=(kc2 == n4 - 1)),
                                inc=(1 if half == 1 else 0))
                            b_ptb[pi].add_u(ylast)
                            b_vtm.add_u(ylast)
                            w = b_ptb[pi].rd() + (pb7.wr() if kc2 == 0 else [])
                            selast = E.op(PE, w, lambda q, kc2=kc2, pi=pi, half=half: q.matmul(
                                S[3][0:1, 0:512], onec_sb[:],
                                ptb[pi][:, half * 512:(half + 1) * 512],
                                start=(kc2 == 0), stop=(kc2 == n4 - 1)),
                                inc=(1 if kc2 == n4 - 1 else 0))
                            b_ptb[pi].add_u(selast)
                        res["y"] = ylast
                        res["se"] = selast

                    stage1(0)
                    if prev is not None:
                        epi_b(prev)
                    stage1(1)
                    for kp in range(2, P):
                        stage1(kp)
                        stage2(kp - 2)
                    stage2(P - 2)
                    stage2(P - 1)
                    PB[yb].set_w(res["y"])
                    PB[7].set_w(res["se"])

                for h in range(16):
                    pass2(h, h - 1 if h > 0 else None)
                epi_a(15)
                epi_b(15)

                for tp in range(2):
                    for ti in range(2):
                        tt = tp * 2 + ti
                        tok = E.op(s_xr[ti], b_xr[ti].wr(),
                                   lambda q, tt=tt, j=j, ti=ti: q.dma_start(
                                       out=xres[ti][:],
                                       in_=d_xrows[j * 512 + tt * 128:j * 512 + (tt + 1) * 128, :]))
                        b_xr[ti].set_w(tok)
                    lasts = {}
                    for yc in range(16):
                        ci = yc % 4
                        if j == 0 and tp == 0:
                            wi = yc % 2
                            tokA = E.op(s_wa[yc % 2], b_ws[wi].wr(),
                                        lambda q, yc=yc, wi=wi: q.dma_start(
                                            out=wstf[wi][:, 0:1024], in_=d_wp[yc, :, 0:1024]))
                            tokB = E.op(s_wg2[yc % 2], b_ws[wi].wr(),
                                        lambda q, yc=yc, wi=wi: q.dma_start(
                                            out=wstf[wi][:, 1024:2048], in_=d_wp[yc, :, 1024:2048]))
                            b_ws[wi].set_w(tokA)
                            b_ws[wi].add_u(tokB)
                            tok2 = E.op(DVE, [tokA, tokB] + b_wqb[ci].wr(),
                                        lambda q, wi=wi, ci=ci: q.tensor_copy(wqb[ci][:], wstf[wi][:]))
                            b_ws[wi].add_u(tok2)
                            b_wqb[ci].set_w(tok2)
                            tokwb = E.op(s_ob[yc % 2], b_wqb[ci].rd(),
                                         lambda q, yc=yc, ci=ci: q.dma_start(
                                             out=d_wpbf[yc], in_=wqb[ci][:]))
                            b_wqb[ci].add_u(tokwb)
                            ready = [tok2]
                        else:
                            tokA = E.op(s_wa[yc % 2], b_wqb[ci].wr(),
                                        lambda q, yc=yc, ci=ci: q.dma_start(
                                            out=wqb[ci][:, 0:1024], in_=d_wpbf[yc, :, 0:1024]))
                            tokB = E.op(s_wg2[yc % 2], b_wqb[ci].wr(),
                                        lambda q, yc=yc, ci=ci: q.dma_start(
                                            out=wqb[ci][:, 1024:2048], in_=d_wpbf[yc, :, 1024:2048]))
                            b_wqb[ci].set_w(tokA)
                            b_wqb[ci].add_u(tokB)
                            ready = [tokA, tokB]
                        tok2r = ready
                        for ti in range(2):
                            tt = tp * 2 + ti
                            for oc in range(4):
                                bi = ti * 4 + oc
                                pb = PB[bi]
                                w = tok2r + (pb.wr() + b_ych.rd() if yc == 0 else [])
                                last = E.op(PE, w, lambda q, yc=yc, tt=tt, oc=oc, bi=bi, ci=ci: q.matmul(
                                    bank(bi), ych[:, yc * 512 + tt * 128:yc * 512 + (tt + 1) * 128],
                                    wqb[ci][:, oc * 512:(oc + 1) * 512],
                                    start=(yc == 0), stop=(yc == 15)),
                                    inc=(1 if (ti == 1 and oc == 3) else 0))
                        b_wqb[ci].add_u(last)
                        b_ych.add_u(last)
                    for ti in range(2):
                        tt = tp * 2 + ti
                        for oc in range(4):
                            bi = ti * 4 + oc
                            pb = PB[bi]
                            pb.set_w(last)
                            tok = E.op(DVE, pb.rd() + b_xr[ti].rd() + (b_x1[ti].wr() if oc == 0 else []),
                                       lambda q, ti=ti, oc=oc, bi=bi: q.tensor_add(
                                           x1pre[ti][:, oc * 512:(oc + 1) * 512], bank(bi),
                                           xres[ti][:, oc * 512:(oc + 1) * 512]))
                            pb.add_u(tok)
                            b_xr[ti].add_u(tok)
                        b_x1[ti].set_w(tok)
                        for ch in range(4):
                            tokb = E.op(DVE, b_x1[ti].rd() + (b_bn.wr() if ch == 0 else []),
                                        lambda q, ti=ti, ch=ch: q.bn_stats(
                                            bnst[:, ch, :], x1pre[ti][:, ch * 512:(ch + 1) * 512]))
                        tokb = E.op(DVE, [tokb], lambda q: q.bn_aggr(bnag[:], bnst[:]))
                        b_bn.set_w(tokb)
                        ve = sm[:, 0:1]
                        nm = sm[:, 1:2]
                        rs = sm[:, 2:3]
                        t1 = E.op(DVE, [tokb] + b_sm[0].wr(), lambda q, ve=ve: q.tensor_scalar_add(
                            ve, bnag[:, 1:2], EPS))
                        t2 = E.op(ACT, [t1], lambda q, ve=ve: q.activation(ve, ve, AF.Ln))
                        t3 = E.op(DVE, [t2], lambda q, ve=ve, rs=rs: q.tensor_scalar_mul(rs, ve, -0.5))
                        t4 = E.op(ACT, [t3], lambda q, rs=rs: q.activation(rs, rs, AF.Exp))
                        t5 = E.op(DVE, [tokb], lambda q, nm=nm: q.tensor_scalar_mul(nm, bnag[:, 0:1], -1.0))
                        b_sm[0].set_w(t4)
                        tok = E.op(DVE, [t4, t5] + b_x1[ti].rd(),
                                   lambda q, ti=ti, nm=nm, rs=rs: q.tensor_scalar(
                                       x1pre[ti][:], x1pre[ti][:], nm, rs, ALU.add, ALU.mult))
                        b_x1[ti].set_w(tok)
                        b_bn.add_u(tok)
                        tok = E.op(s_spw[ti], b_x1[ti].rd(),
                                   lambda q, ti=ti, tt=tt, j=j: q.dma_start(
                                       out=d_spill[j * 512 + tt * 128:j * 512 + (tt + 1) * 128, :],
                                       in_=x1pre[ti][:]))
                        b_x1[ti].add_u(tok)
                        for hv in range(2):
                            tokb16 = E.op(DVE, b_x1[ti].rd() + b_x1b.wr(),
                                          lambda q, ti=ti, hv=hv: q.tensor_copy(
                                              x1bf[:], x1pre[ti][:, hv * 1024:(hv + 1) * 1024]))
                            b_x1[ti].add_u(tokb16)
                            b_x1b.set_w(tokb16)
                            tokwb = E.op(s_spw[ti], b_x1b.rd(),
                                         lambda q, tt=tt, j=j, hv=hv: q.dma_start(
                                             out=d_spill_bf[j * 512 + tt * 128:j * 512 + (tt + 1) * 128,
                                                            hv * 1024:(hv + 1) * 1024],
                                             in_=x1bf[:]))
                            b_x1b.add_u(tokwb)

        def alloc_C():
            st = {}
            es = ExitStack()
            st["_es"] = es
            st["acc"] = es.enter_context(SB("acc", [128, 8 * 2048], F32))
            st["x1ch"] = es.enter_context(SB("x1ch", [128, 16 * 1024], BF16))
            st["hh"] = es.enter_context(SB("hh", [128, 16 * 1024], BF16))
            st["wgf"] = [es.enter_context(SB(f"wgf{i}", [128, 2048], F32)) for i in range(2)]
            st["wuf"] = [es.enter_context(SB(f"wuf{i}", [128, 2048], F32)) for i in range(2)]
            st["wgb"] = [es.enter_context(SB(f"wgb{i}", [128, 2048], BF16)) for i in range(2)]
            st["wub"] = [es.enter_context(SB(f"wub{i}", [128, 2048], BF16)) for i in range(2)]
            st["wd4"] = [es.enter_context(SB(f"wd4_{i}", [128, 1024], F32)) for i in range(3)]
            st["wd4b"] = [es.enter_context(SB(f"wd4b{i}", [128, 1024], BF16)) for i in range(3)]
            st["sgt"] = [es.enter_context(SB(f"sgt{i}", [128, 512], BF16)) for i in range(2)]
            st["wrb"] = es.enter_context(SB("wrb", [128, 128], BF16))
            st["wbuf"] = es.enter_context(SB("wbuf", [128, 64], F32))
            st["rsm"] = es.enter_context(SB("rsm", [128, 16], F32))
            st["prb"] = es.enter_context(SB("prb", [128, 8], F32))
            st["bnst"] = es.enter_context(SB("bnstC", [128, 4, 6], F32))
            st["bnag"] = es.enter_context(SB("bnagC", [128, 2], F32))
            return st

        def walk_C(st):
            PB = [Buf() for _ in range(8)]
            acc, x1ch, hh = st["acc"], st["x1ch"], st["hh"]
            wgf, wuf, wgb, wub = st["wgf"], st["wuf"], st["wgb"], st["wub"]
            wd4, wd4b = st["wd4"], st["wd4b"]
            sgt, wrb, wbuf, rsm = st["sgt"], st["wrb"], st["wbuf"], st["rsm"]
            prb = st["prb"]
            bnst, bnag = st["bnst"], st["bnag"]
            b_acc = Buf()
            b_x1ch = Buf()
            b_hh = Buf()
            b_wgf = [Buf(), Buf()]
            b_wuf = [Buf(), Buf()]
            b_wgb = [Buf(), Buf()]
            b_wub = [Buf(), Buf()]
            b_wd4 = [Buf(), Buf(), Buf()]
            b_wd4b = [Buf(), Buf(), Buf()]
            b_sgt = [Buf(), Buf()]
            b_wrb = Buf()
            b_wbuf = Buf()
            b_rsm = Buf()
            b_prb = Buf()
            b_bn = Buf()

            quiesce()
            tok = E.op(s_cst, [], lambda q: q.dma_start(out=wgf[0][:, 0:128], in_=d_wr))
            b_wgf[0].set_w(tok)
            tok2 = E.op(DVE, b_wgf[0].rd(), lambda q: q.tensor_copy(wrb[:], wgf[0][:, 0:128]))
            b_wgf[0].add_u(tok2)
            b_wrb.set_w(tok2)

            spw_done = [(s_spw[0], s_spw[0].n), (s_spw[1], s_spw[1].n)]
            for H in range(2):
                trt = []
                tr_tt = [[] for _ in range(8)]
                for th in range(16):
                    tt, hv = th // 2, th % 2
                    di = th % 3
                    rows = slice(H * 1024 + tt * 128, H * 1024 + (tt + 1) * 128)
                    tokA = E.op(s_spr[th % 2] if hv == 0 else s_tr[th % 2],
                                list(spw_done) + b_wd4b[di].wr(),
                                lambda q, rows=rows, di=di, hv=hv: q.dma_start(
                                    out=wd4b[di][:],
                                    in_=d_spill_bf[rows, hv * 1024:(hv + 1) * 1024]))
                    b_wd4b[di].set_w(tokA)
                    for k in range(8):
                        ct = hv * 8 + k
                        tok3 = E.op(s_tr[th % 2], [tokA] + (b_x1ch.wr() if th == 0 and k == 0 else []),
                                    lambda q, tt=tt, ct=ct, k=k, di=di: q.dma_start_transpose(
                                        x1ch[:, ct * 1024 + tt * 128:ct * 1024 + (tt + 1) * 128],
                                        wd4b[di][:, k * 128:(k + 1) * 128]))
                        b_wd4b[di].add_u(tok3)
                        tr_tt[tt].append(tok3)
                        trt.append(tok3)
                b_x1ch.set_w(trt[-1])

                ai_toks = []
                for tt in range(8):
                    w = list(spw_done) + (b_acc.wr() if tt < 2 else [])
                    tok = E.op(s_ai[tt % 2], w,
                               lambda q, tt=tt, H=H: q.dma_start(
                                   out=acc[:, tt * 2048:(tt + 1) * 2048],
                                   in_=d_spill[H * 1024 + tt * 128:H * 1024 + (tt + 1) * 128, :]))
                    ai_toks.append(tok)
                b_acc.set_w(ai_toks[-1])
                acc_init = ai_toks

                def router(tt):
                    pb = PB[4]
                    last = None
                    for ct in range(16):
                        w = (pb.wr() + b_wrb.rd() + tr_tt[tt]) if ct == 0 else []
                        last = E.op(PE, w, lambda q, tt=tt, ct=ct: q.matmul(
                            S[0][:, 0:8], x1ch[:, ct * 1024 + tt * 128:ct * 1024 + (tt + 1) * 128],
                            wrb[:, ct * 8:(ct + 1) * 8],
                            start=(ct == 0), stop=(ct == 15)), inc=(1 if ct == 15 else 0))
                    pb.set_w(last)
                    b_wrb.add_u(last)
                    b_x1ch.add_u(last)
                    ng = rsm[:, 0:1]
                    zz = rsm[:, 1:2]
                    m1 = rsm[:, 2:3]
                    m2 = rsm[:, 3:4]
                    rw = rsm[:, 4:5]
                    t1 = E.op(DVE, pb.rd() + b_rsm.wr(), lambda q, ng=ng: q.reduce_max(
                        ng, S[0][:, 0:8], axis=AX.X, negate=True))
                    t2 = E.op(ACT, [t1] + pb.rd() + b_prb.wr(), lambda q, ng=ng, zz=zz: q.activation(
                        prb[:], S[0][:, 0:8], AF.Exp, bias=ng, scale=1.0, accum_out=zz))
                    pb.add_u(t2)
                    b_prb.set_w(t2)
                    t3 = E.op(DVE, [t2], lambda q, zz=zz: q.reciprocal(zz, zz))
                    t4 = E.op(DVE, [t3] + b_prb.rd(), lambda q, zz=zz: q.tensor_scalar_mul(
                        prb[:], prb[:], zz))
                    b_prb.set_w(t4)
                    t5 = E.op(DVE, [t4], lambda q, m1=m1: q.reduce_max(m1, prb[:], axis=AX.X))
                    q2 = rsm[:, 5:13]
                    t6 = E.op(DVE, [t5], lambda q, m1=m1, q2=q2: q.tensor_scalar(
                        q2, prb[:], m1, None, ALU.is_ge))
                    t7 = E.op(DVE, [t6], lambda q, q2=q2: q.scalar_tensor_tensor(
                        q2, q2, -2.0, prb[:], ALU.mult, ALU.add))
                    t8 = E.op(DVE, [t7], lambda q, m2=m2, q2=q2: q.reduce_max(m2, q2, axis=AX.X))
                    t9 = E.op(DVE, [t8], lambda q, m1=m1, m2=m2, rw=rw: q.tensor_add(rw, m1, m2))
                    t10 = E.op(DVE, [t9], lambda q, rw=rw: q.reciprocal(rw, rw))
                    t11 = E.op(DVE, [t8], lambda q, q2=q2, m2=m2: q.tensor_scalar(
                        q2, prb[:], m2, None, ALU.is_ge))
                    t12 = E.op(DVE, [t10, t11], lambda q, q2=q2, rw=rw: q.tensor_scalar_mul(
                        q2, q2, rw))
                    t13 = E.op(DVE, [t12] + b_wbuf.wr() if tt == 0 else [t12],
                               lambda q, tt=tt, q2=q2: q.tensor_mul(
                                   wbuf[:, tt * 8:(tt + 1) * 8], q2, prb[:]))
                    b_prb.add_u(t13)
                    b_rsm.set_w(t13)
                    if tt == 7:
                        b_wbuf.set_w(t13)

                router(0)
                router(1)

                x1g = [tr_tt[0] + tr_tt[1] + tr_tt[2] + tr_tt[3],
                       tr_tt[4] + tr_tt[5] + tr_tt[6] + tr_tt[7]]
                pf = {}

                def prefetch_gu(e, ft):
                    gi = ft % 2
                    tokA = E.op(s_wa[gi], b_wgf[gi].wr(),
                                lambda q, e=e, ft=ft, gi=gi: q.dma_start(
                                    out=wgf[gi][:, 0:1024], in_=d_wg[e, ft, :, 0:1024]))
                    tokB = E.op(s_wg2[gi], b_wgf[gi].wr(),
                                lambda q, e=e, ft=ft, gi=gi: q.dma_start(
                                    out=wgf[gi][:, 1024:2048], in_=d_wg[e, ft, :, 1024:2048]))
                    b_wgf[gi].set_w(tokA)
                    b_wgf[gi].add_u(tokB)
                    tokc = E.op(DVE, [tokA, tokB] + b_wgb[gi].wr(),
                                lambda q, gi=gi: q.tensor_copy(wgb[gi][:], wgf[gi][:]))
                    b_wgf[gi].add_u(tokc)
                    b_wgb[gi].set_w(tokc)
                    tokuA = E.op(s_wb[gi], b_wuf[gi].wr(),
                                 lambda q, e=e, ft=ft, gi=gi: q.dma_start(
                                     out=wuf[gi][:, 0:1024], in_=d_wu[e, ft, :, 0:1024]))
                    tokuB = E.op(s_wu2[gi], b_wuf[gi].wr(),
                                 lambda q, e=e, ft=ft, gi=gi: q.dma_start(
                                     out=wuf[gi][:, 1024:2048], in_=d_wu[e, ft, :, 1024:2048]))
                    b_wuf[gi].set_w(tokuA)
                    b_wuf[gi].add_u(tokuB)
                    tokuc = E.op(ACT, [tokuA, tokuB] + b_wub[gi].wr(),
                                 lambda q, gi=gi: q.copy(wub[gi][:], wuf[gi][:]))
                    b_wuf[gi].add_u(tokuc)
                    b_wub[gi].set_w(tokuc)
                    pf[(e, ft)] = (tokc, tokuc)

                def compute_gu(e, ft):
                    gi = ft % 2
                    tokc, tokuc = pf.pop((e, ft))
                    tokh = None
                    for tc in range(2):
                        pbg, pbu = PB[0], PB[1]
                        lg = lu = None
                        for ct in range(16):
                            wg_ = ([tokc] + pbg.wr() + (x1g[tc] if e == 0 else []) +
                                   (b_hh.wr() if (ft == 0 and tc == 0) else [])) if ct == 0 else []
                            lg = E.op(PE, wg_, lambda q, ct=ct, tc=tc, gi=gi: q.matmul(
                                bank(0), wgb[gi][:, ct * 128:(ct + 1) * 128],
                                x1ch[:, ct * 1024 + tc * 512:ct * 1024 + (tc + 1) * 512],
                                start=(ct == 0), stop=(ct == 15)), inc=(1 if ct == 15 else 0))
                        pbg.set_w(lg)
                        b_wgb[gi].add_u(lg)
                        b_x1ch.add_u(lg)
                        for ct in range(16):
                            wu_ = ([tokuc] + pbu.wr()) if ct == 0 else []
                            lu = E.op(PE, wu_, lambda q, ct=ct, tc=tc, gi=gi: q.matmul(
                                bank(1), wub[gi][:, ct * 128:(ct + 1) * 128],
                                x1ch[:, ct * 1024 + tc * 512:ct * 1024 + (tc + 1) * 512],
                                start=(ct == 0), stop=(ct == 15)), inc=(1 if ct == 15 else 0))
                        pbu.set_w(lu)
                        b_wub[gi].add_u(lu)
                        b_x1ch.add_u(lu)
                        si = tc
                        toksl = E.op(ACT, pbg.rd() + b_sgt[si].wr(),
                                     lambda q, si=si: q.activation(sgt[si][:], bank(0), AF.Silu))
                        pbg.add_u(toksl)
                        b_sgt[si].set_w(toksl)
                        tokh = E.op(DVE, [toksl] + pbu.rd(),
                                    lambda q, ft=ft, tc=tc, si=si: q.tensor_mul(
                                        hh[:, ft * 1024 + tc * 512:ft * 1024 + (tc + 1) * 512],
                                        sgt[si][:], bank(1)))
                        pbu.add_u(tokh)
                        b_sgt[si].add_u(tokh)
                    return tokh

                dpf = {}

                def prefetch_wd(e, cn):
                    di = cn % 3
                    oc, fc = cn // 8, cn % 8
                    dtoks = []
                    for hv in range(2):
                        tok = E.op(s_wd[hv], b_wd4[di].wr(),
                                   lambda q, e=e, oc=oc, fc=fc, di=di, hv=hv: q.dma_start(
                                       out=wd4[di][:, hv * 512:(hv + 1) * 512],
                                       in_=d_wd[e, oc, fc * 2 + hv]))
                        dtoks.append(tok)
                    b_wd4[di].set_w(dtoks[0])
                    b_wd4[di].add_u(dtoks[1])
                    tokc = E.op(DVE, dtoks + b_wd4b[di].wr(),
                                lambda q, di=di: q.tensor_copy(wd4b[di][:], wd4[di][:]))
                    b_wd4[di].add_u(tokc)
                    b_wd4b[di].set_w(tokc)
                    dpf[(e, cn)] = tokc

                def compute_wd(e, cn):
                    di = cn % 3
                    oc, fc = cn // 8, cn % 8
                    tokc = dpf.pop((e, cn))
                    tokmm = None
                    for fi in range(2):
                        ft = fc * 2 + fi
                        for tt in range(8):
                            pb = PB[tt]
                            w = [tokc] + (pb.wr() + b_hh.rd() if ft == 0 else [])
                            tokmm = E.op(PE, w, lambda q, ft=ft, tt=tt, di=di, fi=fi: q.matmul(
                                bank(tt), hh[:, ft * 1024 + tt * 128:ft * 1024 + (tt + 1) * 128],
                                wd4b[di][:, fi * 512:(fi + 1) * 512],
                                start=(ft == 0), stop=(ft == 15)),
                                inc=(1 if tt == 7 else 0))
                        b_hh.add_u(tokmm)
                    b_wd4b[di].add_u(tokmm)
                    if fc == 7:
                        for tt in range(8):
                            pb = PB[tt]
                            pb.set_w(tokmm)
                            wcol = wbuf[:, tt * 8 + e:tt * 8 + e + 1]
                            tok = E.op(DVE, pb.rd() + b_wbuf.rd() + b_acc.rd() + acc_init,
                                       lambda q, tt=tt, oc=oc, wcol=wcol: q.scalar_tensor_tensor(
                                           acc[:, tt * 2048 + oc * 512:tt * 2048 + (oc + 1) * 512],
                                           bank(tt), wcol,
                                           acc[:, tt * 2048 + oc * 512:tt * 2048 + (oc + 1) * 512],
                                           ALU.mult, ALU.add))
                            pb.add_u(tok)
                            b_wbuf.add_u(tok)
                            b_acc.add_u(tok)

                prefetch_gu(0, 0)
                for e in range(NE):
                    for ft in range(16):
                        if ft + 1 < 16:
                            prefetch_gu(e, ft + 1)
                        elif e + 1 < NE:
                            prefetch_gu(e + 1, 0)
                        if ft == 14:
                            prefetch_wd(e, 0)
                        elif ft == 15:
                            prefetch_wd(e, 1)
                        if e == 0 and ft in (2, 4, 6, 8, 10, 12):
                            router(ft // 2 + 1)
                        tokh = compute_gu(e, ft)
                    b_hh.set_w(tokh)
                    for cn in range(32):
                        if cn + 2 < 32:
                            prefetch_wd(e, cn + 2)
                        compute_wd(e, cn)

                acc_users = list(b_acc.u)
                for tt in range(8):
                    arow = acc[:, tt * 2048:(tt + 1) * 2048]
                    for ch in range(4):
                        tokb = E.op(DVE, (acc_users if ch == 0 else []) + (b_bn.wr() if ch == 0 else []),
                                    lambda q, tt=tt, ch=ch, arow=arow: q.bn_stats(
                                        bnst[:, ch, :], arow[:, ch * 512:(ch + 1) * 512]))
                    tokb = E.op(DVE, [tokb], lambda q: q.bn_aggr(bnag[:], bnst[:]))
                    b_bn.set_w(tokb)
                    ve = rsm[:, 13:14]
                    nm = rsm[:, 14:15]
                    t1 = E.op(DVE, [tokb] + b_rsm.wr(), lambda q, ve=ve: q.tensor_scalar_add(
                        ve, bnag[:, 1:2], EPS))
                    t2 = E.op(ACT, [t1], lambda q, ve=ve: q.activation(ve, ve, AF.Ln))
                    t3 = E.op(DVE, [t2], lambda q, ve=ve: q.tensor_scalar_mul(ve, ve, -0.5))
                    t4 = E.op(ACT, [t3], lambda q, ve=ve: q.activation(ve, ve, AF.Exp))
                    t5 = E.op(DVE, [tokb], lambda q, nm=nm: q.tensor_scalar_mul(nm, bnag[:, 0:1], -1.0))
                    b_rsm.set_w(t5)
                    tok = E.op(DVE, [t4, t5], lambda q, arow=arow, nm=nm, ve=ve: q.tensor_scalar(
                        arow, arow, nm, ve, ALU.add, ALU.mult))
                    b_bn.add_u(tok)
                    tokd = E.op(s_ob[tt % 2], [tok], lambda q, tt=tt, H=H, arow=arow: q.dma_start(
                        out=d_out[H * 1024 + tt * 128:H * 1024 + (tt + 1) * 128, :], in_=arow))
                    b_acc.add_u(tokd)

            E.wait_only(s_ob[0], [(s_ob[0], s_ob[0].n), (s_ob[1], s_ob[1].n)])

        with nc.Block() as blkA:
            @blkA.tensor
            def _(q):
                E.begin("PE", q); walk_A()
            @blkA.vector
            def _(q):
                E.begin("DVE", q); walk_A()
            @blkA.scalar
            def _(q):
                E.begin("ACT", q); walk_A()
            @blkA.sync
            def _(q):
                E.begin("SP", q); walk_A()
            @blkA.gpsimd
            def _(q):
                E.begin("PL", q); walk_A()

        esA.close()
        totals_A = [eq.n for eq in E.eqs]

        def resume(mode, q, totals):
            E.mode = mode
            E.q = q
            for eq, n in zip(E.eqs, totals):
                eq.n = n
                eq.waited = {}

        stB = alloc_B()
        with stB["_es"]:
            with nc.Block() as blkB:
                @blkB.tensor
                def _(q):
                    resume("PE", q, totals_A); walk_B(stB)
                @blkB.vector
                def _(q):
                    resume("DVE", q, totals_A); walk_B(stB)
                @blkB.scalar
                def _(q):
                    resume("ACT", q, totals_A); walk_B(stB)
                @blkB.sync
                def _(q):
                    resume("SP", q, totals_A); walk_B(stB)
                @blkB.gpsimd
                def _(q):
                    resume("PL", q, totals_A); walk_B(stB)
        totals_B = [eq.n for eq in E.eqs]
        esAB.close()

        stC = alloc_C()
        with stC["_es"]:
            with nc.Block() as blkC:
                @blkC.tensor
                def _(q):
                    resume("PE", q, totals_B); walk_C(stC)
                @blkC.vector
                def _(q):
                    resume("DVE", q, totals_B); walk_C(stC)
                @blkC.scalar
                def _(q):
                    resume("ACT", q, totals_B); walk_C(stC)
                @blkC.sync
                def _(q):
                    resume("SP", q, totals_B); walk_C(stC)
                @blkC.gpsimd
                def _(q):
                    resume("PL", q, totals_B); walk_C(stC)
    return nc


def _tables():
    inv = (10000.0 ** (-np.arange(0, HD, 2, dtype=np.float64) / HD)).astype(np.float64)
    tpos = np.arange(T, dtype=np.float64)
    fr = np.outer(tpos, inv)
    emb = np.concatenate([fr, fr], axis=1)
    cos = np.cos(emb).T.astype(BFD)
    sin = np.sin(emb).T.astype(BFD)
    return np.ascontiguousarray(cos), np.ascontiguousarray(sin)


def _masks(half):
    cs = CHUNK_SETS[half]
    m2 = np.zeros((NSLOT, 8, 128, 512), dtype=np.float32)
    for j in range(NSLOT):
        g = cs[j]
        n = NK[j]
        for kc2 in range(8):
            kabs = (4 * (n - 2)) + kc2
            kg = kabs * 128 + np.arange(128)[:, None]
            qg = g * 512 + np.arange(512)[None, :]
            m2[j, kc2] = np.where(kg <= qg, 1.0, 0.0)
    return m2.astype(BFD)


_NC_CACHE = {}


def kernel(L0_attn_x, L0_attn_Wq, L0_attn_Wk, L0_attn_Wv, L0_attn_Wproj,
           L0_ffn_Wr, L0_ffn_Wg, L0_ffn_Wu, L0_ffn_Wd):
    x = np.ascontiguousarray(np.asarray(L0_attn_x, dtype=np.float32))
    Wq = np.asarray(L0_attn_Wq, dtype=np.float32)
    Wk = np.asarray(L0_attn_Wk, dtype=np.float32)
    Wv = np.asarray(L0_attn_Wv, dtype=np.float32)
    Wp = np.asarray(L0_attn_Wproj, dtype=np.float32)
    Wr = np.asarray(L0_ffn_Wr, dtype=np.float32)
    Wg = np.asarray(L0_ffn_Wg, dtype=np.float32)
    Wu = np.asarray(L0_ffn_Wu, dtype=np.float32)
    Wd = np.asarray(L0_ffn_Wd, dtype=np.float32)

    cos, sin = _tables()
    wq_t = np.ascontiguousarray(
        Wq.reshape(16, 128, 16, 128).transpose(2, 1, 0, 3).reshape(16, 128, 2048))
    wk_t = np.ascontiguousarray(
        Wk.reshape(16, 128, 4, 128).transpose(1, 2, 0, 3).reshape(128, 8192))
    wv_t = np.ascontiguousarray(Wv.reshape(16, 128, 512).transpose(1, 0, 2).reshape(128, 8192))
    wp_t = np.ascontiguousarray(Wp.reshape(16, 128, 2048))
    wr_t = np.ascontiguousarray(Wr.reshape(16, 128, 8).transpose(1, 0, 2).reshape(128, 128))
    wg_t = np.ascontiguousarray(
        Wg.reshape(NE, 16, 128, 16, 128).transpose(0, 3, 2, 1, 4).reshape(NE, 16, 128, 2048))
    wu_t = np.ascontiguousarray(
        Wu.reshape(NE, 16, 128, 16, 128).transpose(0, 3, 2, 1, 4).reshape(NE, 16, 128, 2048))
    wd_t = np.ascontiguousarray(
        Wd.reshape(NE, 16, 128, 4, 512).transpose(0, 3, 1, 2, 4))
    ones_f = np.ones((1, 128), dtype=np.float32)
    ones_c = np.ones((128, 1), dtype=BFD)
    konst = np.stack([np.full(128, -SHIFT, np.float32), np.full(128, SHIFT, np.float32)], axis=1)

    in_maps = []
    row_sets = []
    for c in range(NCORE):
        b, half = c // 2, c % 2
        cs = CHUNK_SETS[half]
        rows = np.concatenate([np.arange(g * 512, (g + 1) * 512) for g in cs])
        row_sets.append((b, rows))
        xb = x[b]
        xT = np.ascontiguousarray(xb.T)
        m2 = _masks(half)
        in_maps.append({
            "xT": xT,
            "xqT": np.ascontiguousarray(xT[:, rows]),
            "xrows": np.ascontiguousarray(xb[rows]),
            "cosk": cos, "sink": sin,
            "cosq": np.ascontiguousarray(cos[:, rows]),
            "sinq": np.ascontiguousarray(sin[:, rows]),
            "maskp2": m2,
            "wq_t": wq_t, "wk_t": wk_t, "wv_t": wv_t, "wp_t": wp_t, "wr_t": wr_t,
            "wg_t": wg_t, "wu_t": wu_t, "wd_t": wd_t,
            "ones_f": ones_f, "ones_c": ones_c, "konst": konst,
        })

    if "nc" not in _NC_CACHE:
        _NC_CACHE["nc"] = build_nc()
    nc = _NC_CACHE["nc"]
    res = run_bass_kernel_spmd(nc, in_maps, core_ids=list(range(NCORE)))
    out = np.empty((B, T, C), dtype=np.float32)
    for c in range(NCORE):
        b, rows = row_sets[c]
        out[b][rows] = np.asarray(res.results[c]["out_rows"], dtype=np.float32)
    return out
